# version 03
# baseline (speedup 1.0000x reference)
import numpy as np
from contextlib import ExitStack

import concourse.bass as bass
import concourse.mybir as mybir
from concourse.bass_utils import run_bass_kernel_spmd

F32 = mybir.dt.float32
BF16 = mybir.dt.bfloat16
FP8 = mybir.dt.float8e4
AF = mybir.ActivationFunctionType
OP = mybir.AluOpType

B, L, D, H = 16, 4096, 1024, 4096
NCORES = 8
BPC = B // NCORES
NG = D // 128
NHC = H // 128
TSEG = 512
NSEG_PER_B = L // TSEG
FS = 256
NSUB = TSEG // FS
CGN = 4
NCG = NHC // CGN

LN2 = float(np.log(2.0))
INV_LN2 = 1.0 / LN2
M32 = 1.5 * 2.0 ** 23
M64 = 1.5 * 2.0 ** 52
PIH = float(np.pi / 2.0)

C_LNBIAS_H = 0
C_NEGLN2 = 1
C_POSLN2 = 2
C_9LN2 = 3
C_N9LN2 = 4
C_24LN2 = 5
C_N24LN2 = 6
C_LNBIAS_O = 7
C_PIH = 8
C_NEG1 = 9
C_LNEPS = 10
NCONST = 16

CONST_VALUES = np.zeros(NCONST, dtype=np.float32)
CONST_VALUES[C_LNBIAS_H] = 2.0 ** -45
CONST_VALUES[C_NEGLN2] = -LN2
CONST_VALUES[C_POSLN2] = LN2
CONST_VALUES[C_9LN2] = 9.0 * LN2
CONST_VALUES[C_N9LN2] = -9.0 * LN2
CONST_VALUES[C_24LN2] = 24.0 * LN2
CONST_VALUES[C_N24LN2] = -24.0 * LN2
CONST_VALUES[C_LNBIAS_O] = 2.0 ** -60
CONST_VALUES[C_PIH] = PIH
CONST_VALUES[C_NEG1] = -1.0
CONST_VALUES[C_LNEPS] = 1e-5

ENGS = ("sy", "ve", "sc", "gp", "te")
SEMS = ("ld", "st", "v", "a", "g", "t")
SERIAL = False
ENG_SEM = {"ve": "v", "sc": "a", "gp": "g", "te": "t"}


class Plan:

    def __init__(self):
        self.q = {e: [] for e in ENGS}
        self.cnt = {s: 0 for s in SEMS}
        self.region = "startup"

    def add(self, eng, fn, waits=(), inc=None, n=1):
        region_ok = True
        if isinstance(SERIAL, (set, list)):
            regs = [x.split("@")[1] for x in SERIAL if "@" in x]
            if regs:
                region_ok = self.region in regs
        if region_ok and SERIAL and eng in ENG_SEM and (SERIAL is True or any(x.split("@")[0].startswith(eng) for x in (SERIAL if isinstance(SERIAL, (set, list)) else []))):
            sel = None
            if isinstance(SERIAL, (set, list)):
                for x in SERIAL:
                    x0 = x.split("@")[0]
                    if x0.startswith(eng):
                        sel = x0[len(eng):]
            semset = ("v", "a", "t", "g") if (SERIAL is True or not sel) else tuple(sel)
            waits = list(waits) + [(s, self.cnt[s]) for s in semset if s != ENG_SEM.get(eng)]
            if inc is None:
                inc = ENG_SEM[eng]
        w = {}
        for ev in waits:
            if ev is None:
                continue
            if isinstance(ev, list):
                for e2 in ev:
                    if e2 is not None:
                        s, val = e2
                        w[s] = max(w.get(s, 0), val)
                continue
            s, val = ev
            w[s] = max(w.get(s, 0), val)
        ev_out = None
        if inc is not None:
            self.cnt[inc] += n
            ev_out = (inc, self.cnt[inc])
        self.q[eng].append((fn, sorted(w.items()), inc, n))
        return ev_out

    def dma(self, eng, out_ap, in_ap, waits=(), sem="ld"):
        return self.add(eng, lambda e, o=out_ap, i=in_ap: e.dma_start(out=o, in_=i),
                        waits=waits, inc=sem, n=16)

    def nop(self, eng, waits):
        sem = {"ve": "v", "sc": "a", "gp": "g", "te": "t"}[eng]
        return self.add(eng, lambda e: e.engine_nop(), waits=waits, inc=sem)


def emit_quant(pl, src, dst, width, signed, e4m3, cst, kt, s2, s3, s4, ready_ev):
    wshift = 6.0 if e4m3 else 14.0
    c1 = INV_LN2 * (0.5 if signed else 1.0)
    ci_p = C_9LN2 if e4m3 else C_24LN2
    ci_n = C_N9LN2 if e4m3 else C_N24LN2
    sl = lambda t: t[:, 0:width]
    if signed:
        e_sq = pl.add("sc", lambda e: e.activation(sl(s2), src, AF.Square, bias=0.0, scale=1.0),
                      waits=[ready_ev], inc="a")
        e_ln = pl.add("sc", lambda e: e.activation(sl(s2), sl(s2), AF.Ln,
                                                   bias=cst[:, C_LNBIAS_O:C_LNBIAS_O + 1], scale=1.0),
                      waits=[e_sq], inc="a")
    else:
        e_ln = pl.add("sc", lambda e: e.activation(sl(s2), src, AF.Ln,
                                                   bias=cst[:, C_LNBIAS_H:C_LNBIAS_H + 1], scale=1.0),
                      waits=[ready_ev], inc="a")
    e_t1 = pl.add("ve", lambda e: e.tensor_scalar(sl(s3), sl(s2), c1, wshift - 0.5, OP.mult, OP.add),
                  waits=[e_ln, ready_ev], inc="v")
    e_A1 = pl.add("ve", lambda e: e.tensor_scalar(sl(s4), sl(s3), M32, M32, OP.add, OP.max), waits=[], inc="v")
    e_A = pl.add("ve", lambda e: e.tensor_scalar(sl(s4), sl(s4), M32, None, OP.subtract), waits=[], inc="v")
    e_D1 = pl.add("ve", lambda e: e.tensor_scalar(sl(s3), sl(s3), M64, M64, OP.add, OP.max), waits=[], inc="v")
    e_D = pl.add("ve", lambda e: e.tensor_scalar(sl(s3), sl(s3), M64, None, OP.subtract), waits=[], inc="v")
    e_fl = pl.add("ve", lambda e: e.scalar_tensor_tensor(sl(s2), sl(s4), kt, sl(s3), OP.mult, OP.add),
                  waits=[], inc="v")
    e_si = pl.add("sc", lambda e: e.activation(sl(s3), sl(s2), AF.Exp,
                                               bias=cst[:, ci_p:ci_p + 1], scale=cst[:, C_NEGLN2:C_NEGLN2 + 1]),
                  waits=[e_fl], inc="a")
    e_s = pl.add("sc", lambda e: e.activation(sl(s4), sl(s2), AF.Exp,
                                              bias=cst[:, ci_n:ci_n + 1], scale=cst[:, C_POSLN2:C_POSLN2 + 1]),
                 waits=[e_fl], inc="a")
    e_y = pl.add("ve", lambda e: e.tensor_tensor(sl(s2), src, sl(s3), OP.mult),
                 waits=[e_si, e_s], inc="v")
    e_rA = pl.add("ve", lambda e: e.tensor_scalar(sl(s3), sl(s2), M32, M32, OP.add, OP.subtract),
                  waits=[], inc="v")
    e_rD = pl.add("ve", lambda e: e.tensor_scalar(sl(s2), sl(s2), M64, M64, OP.add, OP.subtract),
                  waits=[], inc="v")
    e_r = pl.add("ve", lambda e: e.scalar_tensor_tensor(sl(s3), sl(s3), kt, sl(s2), OP.mult, OP.add),
                 waits=[], inc="v")
    e_q = pl.add("ve", lambda e: e.tensor_tensor(dst, sl(s3), sl(s4), OP.mult),
                 waits=[e_s], inc="v")
    return e_q


def build_nc(upto=None, tap_si=0, tap_b=0):
    nc = bass.Bass("TRN2", target_bir_lowering=False)

    x_d = nc.dram_tensor("x", [D, BPC, L], F32, kind="ExternalInput").ap()
    w1t_d = nc.dram_tensor("w1t", [D, H], F32, kind="ExternalInput").ap()
    w2t_d = nc.dram_tensor("w2t", [H, D], F32, kind="ExternalInput").ap()
    b1c_d = nc.dram_tensor("b1c", [128, NHC], F32, kind="ExternalInput").ap()
    b2c_d = nc.dram_tensor("b2c", [128, NG], F32, kind="ExternalInput").ap()
    ang_d = nc.dram_tensor("angle", [128, NG], F32, kind="ExternalInput").ap()
    lnm_d = nc.dram_tensor("lnml", [128, NG], F32, kind="ExternalInput").ap()
    scl_d = nc.dram_tensor("scl", [128, NG], F32, kind="ExternalInput").ap()
    ire_d = nc.dram_tensor("ire", [128, NG], F32, kind="ExternalInput").ap()
    iim_d = nc.dram_tensor("iim", [128, NG], F32, kind="ExternalInput").ap()
    lre_d = nc.dram_tensor("lre", [128, NG], F32, kind="ExternalInput").ap()
    lim_d = nc.dram_tensor("lim", [128, NG], F32, kind="ExternalInput").ap()
    cst_d = nc.dram_tensor("cst", [128, NCONST], F32, kind="ExternalInput").ap()
    io8_d = nc.dram_tensor("iota8", [128, 8], F32, kind="ExternalInput").ap()
    one_d = nc.dram_tensor("onec", [128, 1], F32, kind="ExternalInput").ap()
    oner_d = nc.dram_tensor("oner", [1, 128], F32, kind="ExternalInput").ap()
    out_d = nc.dram_tensor("out", [D, BPC, L], F32, kind="ExternalOutput").ap()

    with ExitStack() as es:
        sb = lambda name, shape, dt: es.enter_context(nc.sbuf_tensor(name, shape, dt))
        psal = lambda name, shape: es.enter_context(nc.psum_tensor(name, shape, F32))

        w1b = [sb(f"w1b{g}", [128, H], BF16) for g in range(NG)]
        w2q = [sb(f"w2q{c}", [128, D], FP8) for c in range(NHC)]
        xseg = [sb(f"xseg{g}", [128, TSEG], F32) for g in range(NG)]
        wr = [sb(f"wr{g}", [128, TSEG], BF16) for g in range(NG)]
        wi = [sb(f"wi{g}", [128, TSEG], BF16) for g in range(NG)]
        bc = [sb(f"bc{g}", [128, TSEG], BF16) for g in range(NG)]
        bs = [sb(f"bs{g}", [128, TSEG], BF16) for g in range(NG)]
        xb = [sb(f"xb{g}", [128, TSEG], BF16) for g in range(NG)]
        h4 = sb("h4", [128, CGN * FS], F32)
        hq = sb("hq", [128, NHC * FS], FP8)
        oq = sb("oq", [128, NG * FS], F32)
        s1 = sb("s1", [128, 1024], F32)
        s2 = sb("s2", [128, 1024], F32)
        s3 = sb("s3", [128, 1024], F32)
        s4 = sb("s4", [128, 1024], F32)

        b1c = sb("b1cs", [128, NHC], F32)
        b2c = sb("b2cs", [128, NG], F32)
        bqt = sb("bqt", [128, NG], F32)
        ang = sb("angs", [128, NG], F32)
        lnm = sb("lnms", [128, NG], F32)
        scl = sb("scls", [128, NG], F32)
        ire = sb("ires", [128, NG], F32)
        iim = sb("iims", [128, NG], F32)
        lre = sb("lres", [128, NG], F32)
        lim = sb("lims", [128, NG], F32)
        cst = sb("csts", [128, NCONST], F32)
        io8 = sb("io8s", [128, 8], F32)
        onec = sb("onecs", [128, 1], F32)
        oner = sb("oners", [1, 128], F32)
        kt_t = sb("kt", [128, 1], F32)
        onebc = sb("onebc", [128, 1], BF16)

        dec = sb("dec", [128, NG], F32)
        rho = sb("rho", [128, NG], F32)
        gte = sb("gte", [128, NG], F32)
        phi = sb("phi", [128, NG], F32)
        rc = {n: sb(f"rc{n}", [128, NG], F32) for n in (8, 16, 32, 64, 128, 256, 512)}
        rs = {n: sb(f"rs{n}", [128, NG], F32) for n in (8, 16, 32, 64, 128, 256, 512)}
        src_t = sb("srcos", [128, NG], F32)
        srs_t = sb("srsin", [128, NG], F32)
        pa_t = sb("pa", [128, NG], F32)
        pb_t = sb("pb", [128, NG], F32)
        qa_t = sb("qa", [128, NG], F32)
        qb_t = sb("qb", [128, NG], F32)
        tmp8a = sb("tmp8a", [128, NG], F32)
        tmp8b = sb("tmp8b", [128, NG], F32)
        vsp = sb("vsp", [128, NG], F32)
        vcp = sb("vcp", [128, NG], F32)
        vcr = sb("vcr", [128, NG * BPC], F32)
        vci = sb("vci", [128, NG * BPC], F32)
        stm = sb("stm", [1, TSEG], F32)
        stq = sb("stq", [1, TSEG], F32)
        stv = sb("stv", [1, TSEG], F32)
        str_ = sb("strs", [1, TSEG], F32)
        stc = sb("stc", [1, TSEG], F32)

        psA = psal("psA", [128, CGN * FS])
        psB = psal("psB", [128, 4 * FS])
        psBC = psal("psBC", [128, 1024])
        psX = psal("psX", [1, TSEG])
        psQ = psal("psQ", [1, TSEG])

        sem = {s: es.enter_context(nc.semaphore(s)) for s in SEMS}
        block = es.enter_context(nc.Block())

        pl = Plan()
        kt = kt_t[:, 0:1]

        ev_small = None
        for t_sb, t_d in [(b1c, b1c_d), (b2c, b2c_d), (ang, ang_d), (lnm, lnm_d),
                          (scl, scl_d), (ire, ire_d), (iim, iim_d), (lre, lre_d),
                          (lim, lim_d), (io8, io8_d), (onec, one_d), (oner, oner_d),
                          (cst, cst_d)]:
            ev_small = pl.dma("sy", t_sb[:], t_d)

        e = pl.add("ve", lambda e: e.tensor_scalar(kt_t[:], onec[:], 2.0 ** -30, None, OP.add),
                   waits=[ev_small], inc="v")
        e = pl.add("ve", lambda e: e.tensor_scalar(kt_t[:], kt_t[:], 1.0, -(2.0 ** 30), OP.subtract, OP.mult),
                   waits=[e], inc="v")
        ev_kt = pl.add("ve", lambda e: e.tensor_scalar(kt_t[:], kt_t[:], 1.0, None, OP.add),
                       waits=[e], inc="v")
        ev_onebc = pl.add("ve", lambda e: e.tensor_copy(onebc[:], onec[:]), waits=[ev_small], inc="v")

        e_dec = pl.add("sc", lambda e: e.activation(dec[:], lnm[:], AF.Exp, bias=0.0, scale=1.0),
                       waits=[ev_small], inc="a")
        e_dsc = pl.add("ve", lambda e: e.tensor_tensor(tmp8a[:], dec[:], scl[:], OP.mult),
                       waits=[e_dec], inc="v")
        e_rho = pl.add("sc", lambda e: e.activation(rho[:], tmp8a[:], AF.Exp, bias=0.0,
                                                    scale=cst[:, C_NEG1:C_NEG1 + 1]),
                       waits=[e_dsc], inc="a")
        e_gte = pl.add("ve", lambda e: e.tensor_scalar(gte[:], rho[:], -1.0, 1.0, OP.mult, OP.add),
                       waits=[e_rho], inc="v")
        e_phi = pl.add("ve", lambda e: e.tensor_tensor(phi[:], ang[:], scl[:], OP.mult),
                       waits=[ev_small], inc="v")
        e1 = pl.add("ve", lambda e: e.tensor_scalar(tmp8b[:], phi[:], 8.0, None, OP.mult),
                    waits=[e_phi, e_dsc], inc="v")
        e_s8 = pl.add("sc", lambda e: e.activation(rs[8][:], tmp8b[:], AF.Sin, bias=0.0, scale=1.0),
                      waits=[e1], inc="a")
        e_c8 = pl.add("sc", lambda e: e.activation(rc[8][:], tmp8b[:], AF.Sin,
                                                   bias=cst[:, C_PIH:C_PIH + 1], scale=1.0),
                      waits=[e1], inc="a")
        prev_c, prev_s = rc[8], rs[8]
        last_sq = [e_s8, e_c8, e_gte]
        for n in (16, 32, 64, 128, 256, 512):
            pc, ps_ = prev_c, prev_s
            ecs = pl.add("ve", lambda e, pc=pc, ps_=ps_: e.tensor_tensor(tmp8a[:], pc[:], ps_[:], OP.mult),
                         waits=last_sq, inc="v")
            ecc = pl.add("ve", lambda e, pc=pc: e.tensor_tensor(tmp8b[:], pc[:], pc[:], OP.mult),
                         waits=last_sq, inc="v")
            ec2 = pl.add("ve", lambda e, n=n: e.tensor_scalar(rc[n][:], tmp8b[:], 2.0, 1.0, OP.mult, OP.subtract),
                         waits=[ecc], inc="v")
            es2 = pl.add("ve", lambda e, n=n: e.tensor_scalar(rs[n][:], tmp8a[:], 2.0, None, OP.mult),
                         waits=[ecs], inc="v")
            last_sq = [ec2, es2]
            prev_c, prev_s = rc[n], rs[n]
        ev_rot_done = last_sq

        e_srcontainerc = pl.add("ve", lambda e: e.tensor_scalar(src_t[:], ang[:], 0.0, 1.0, OP.mult, OP.add),
                                waits=[ev_small], inc="v")
        e_srs0 = pl.add("ve", lambda e: e.tensor_scalar(srs_t[:], ang[:], 0.0, None, OP.mult),
                        waits=[ev_small], inc="v")
        ev_sr = [e_srcontainerc] if False else [e_srcontainerc, e_srs0]

        e_sp = pl.add("sc", lambda e: e.activation(vsp[:], phi[:], AF.Sin, bias=0.0, scale=1.0),
                      waits=[e_phi, e_c8], inc="a")
        e_cp = pl.add("sc", lambda e: e.activation(vcp[:], phi[:], AF.Sin,
                                                   bias=cst[:, C_PIH:C_PIH + 1], scale=1.0),
                      waits=[e_phi, e1, e_sp], inc="a")
        sA = s1[:, 0:NG]
        sB = s1[:, NG:2 * NG]
        sC = s1[:, 2 * NG:3 * NG]
        sD = s1[:, 3 * NG:4 * NG]
        e_a = pl.add("ve", lambda e: e.tensor_tensor(sA, vcp[:], lre[:], OP.mult),
                     waits=[e_cp, ev_small], inc="v")
        e_b = pl.add("ve", lambda e: e.tensor_tensor(sB, vsp[:], lim[:], OP.mult),
                     waits=[e_cp], inc="v")
        e_c = pl.add("ve", lambda e: e.tensor_tensor(sC, vsp[:], lre[:], OP.mult),
                     waits=[e_cp], inc="v")
        e_d = pl.add("ve", lambda e: e.tensor_tensor(sD, vcp[:], lim[:], OP.mult),
                     waits=[e_cp], inc="v")
        ev_vinit = []
        for b_ in range(BPC):
            evr = pl.add("ve", lambda e, b_=b_: e.tensor_tensor(
                vcr[:, b_ * NG:(b_ + 1) * NG], sA, sB, OP.subtract), waits=[e_a, e_b], inc="v")
            evi = pl.add("ve", lambda e, b_=b_: e.tensor_tensor(
                vci[:, b_ * NG:(b_ + 1) * NG], sC, sD, OP.add), waits=[e_c, e_d], inc="v")
            ev_vinit.append((evr, evi))
        ev_vinit_done = [ev_vinit[-1][0], ev_vinit[-1][1]]

        ev_base_last = None
        for g in range(NG):
            phig = phi[:, g:g + 1]
            dblc = s2[:, 0:TSEG]
            dbls = s3[:, 0:TSEG]
            w0 = [e_phi, ev_small] + ev_rot_done
            if ev_base_last is not None:
                w0.append(ev_base_last)
            e_m = pl.add("sc", lambda e, phig=phig: e.activation(s4[:, 0:8], io8[:], AF.Identity,
                                                                 bias=0.0, scale=phig),
                         waits=w0, inc="a")
            e_ss = pl.add("sc", lambda e: e.activation(dbls[:, 0:8], s4[:, 0:8], AF.Sin, bias=0.0, scale=1.0),
                          waits=[e_m], inc="a")
            e_cc = pl.add("sc", lambda e: e.activation(dblc[:, 0:8], s4[:, 0:8], AF.Sin,
                                                       bias=cst[:, C_PIH:C_PIH + 1], scale=1.0),
                          waits=[e_m], inc="a")
            last = [e_ss, e_cc]
            n = 8
            while n < TSEG:
                rcn = rc[n][:, g:g + 1]
                rsn = rs[n][:, g:g + 1]
                e_t = pl.add("ve", lambda e, n=n, rsn=rsn, dbls=dbls: e.tensor_scalar(
                    s4[:, 0:n], dbls[:, 0:n], rsn, None, OP.mult), waits=last, inc="v")
                e_nc = pl.add("ve", lambda e, n=n, rcn=rcn, dblc=dblc: e.scalar_tensor_tensor(
                    dblc[:, n:2 * n], dblc[:, 0:n], rcn, s4[:, 0:n], OP.mult, OP.subtract),
                    waits=[e_t], inc="v")
                e_t2 = pl.add("ve", lambda e, n=n, rcn=rcn, dbls=dbls: e.tensor_scalar(
                    s4[:, 0:n], dbls[:, 0:n], rcn, None, OP.mult), waits=[e_nc], inc="v")
                e_ns = pl.add("ve", lambda e, n=n, rsn=rsn, dblc=dblc, dbls=dbls: e.scalar_tensor_tensor(
                    dbls[:, n:2 * n], dblc[:, 0:n], rsn, s4[:, 0:n], OP.mult, OP.add),
                    waits=[e_t2], inc="v")
                last = [e_nc, e_ns]
                n *= 2
            e_bcg = pl.add("ve", lambda e, g=g, dblc=dblc: e.tensor_copy(bc[g][:], dblc), waits=last, inc="v")
            e_bsg = pl.add("ve", lambda e, g=g, dbls=dbls: e.tensor_copy(bs[g][:], dbls), waits=[e_bcg], inc="v")
            ev_base_last = e_bsg
        ev_base_done = ev_base_last

        if upto == "tbl":
            for g in range(NG):
                pl.dma("gp", out_d[g * 128:(g + 1) * 128, 0, 0:TSEG], bc[g][:], waits=[ev_base_done], sem="st")
                pl.dma("gp", out_d[g * 128:(g + 1) * 128, 0, TSEG:2 * TSEG], bs[g][:], waits=[ev_base_done], sem="st")
            total_store = pl.cnt["st"]
            pl.add("sy", None, waits=[("st", total_store)])
            def run_queue(eng_obj, name):
                hw = {s: 0 for s in SEMS}
                for (fn, waits, inc, n) in pl.q[name]:
                    for (s, val) in waits:
                        if val > hw[s]:
                            eng_obj.wait_ge(sem[s], val)
                            hw[s] = val
                    if fn is None:
                        continue
                    r = fn(eng_obj)
                    if inc is not None:
                        r.then_inc(sem[inc], n)
            @block.sync
            def _(sync):
                run_queue(sync, "sy")
            @block.vector
            def _(vector):
                run_queue(vector, "ve")
            @block.scalar
            def _(scalar):
                run_queue(scalar, "sc")
            @block.gpsimd
            def _(gpsimd):
                run_queue(gpsimd, "gp")
            @block.tensor
            def _(tensor):
                run_queue(tensor, "te")
            return nc

        pl.region = "su2"
        ev_w1 = ev_base_done
        for g in range(NG):
            for qtr in range(4):
                ev_l = pl.dma("sy", s2[:, 0:1024],
                              w1t_d[g * 128:(g + 1) * 128, qtr * 1024:(qtr + 1) * 1024],
                              waits=[ev_w1])
                ev_w1 = pl.add("ve", lambda e, g=g, qtr=qtr: e.tensor_copy(
                    w1b[g][:, qtr * 1024:(qtr + 1) * 1024], s2[:, 0:1024]), waits=[ev_l], inc="v")

        ev_w2 = None
        ev_src_free = pl.nop("ve", [ev_w1, ev_kt, ev_vinit_done[0], ev_vinit_done[1]])
        for c in range(NHC):
            ev_l = pl.dma("sy", s1[:, 0:1024], w2t_d[c * 128:(c + 1) * 128, :], waits=[ev_src_free])
            ready = pl.nop("ve", [ev_l])
            ev_w2 = emit_quant(pl, s1[:, 0:1024], w2q[c][:], 1024, True, True,
                               cst, kt, s2, s3, s4, ready)
            ev_src_free = ev_w2
        ready = pl.nop("ve", [ev_w2, ev_small])
        ev_bq = emit_quant(pl, b2c[:], bqt[:], NG, True, False, cst, kt, s2, s3, s4, ready)

        pl.region = "main"
        ev_xfree = ev_bq
        ev_xb_free = ev_bq
        ev_psA_free = None
        ev_psB_free = None
        ev_psBC_free = None
        ev_hq_free = None
        ev_oq_free = None
        ev_psX_free = None
        ev_psQ_free = None
        ev_h4_free = None
        ev_sr_prep = None
        ev_rotors = []
        ev_carry = {}
        for g in range(NG):
            for b_ in range(BPC):
                ev_carry[("r", g, b_)] = ev_vinit[b_][0]
                ev_carry[("i", g, b_)] = ev_vinit[b_][1]
        ev_rot_free = [None] * NG
        ev_scratch = ev_bq

        segs = [(si, b_) for si in range(NSEG_PER_B) for b_ in range(BPC)]
        for (si, b_) in segs:
            t0 = si * TSEG
            if b_ == 0:
                w8 = [ev_kt, e_gte] + ev_sr
                if ev_sr_prep is not None:
                    w8.append(ev_sr_prep)
                e_ = pl.add("ve", lambda e: e.tensor_tensor(tmp8a[:], ire[:], src_t[:], OP.mult),
                            waits=w8 + ev_rot_done, inc="v")
                e_ = pl.add("ve", lambda e: e.tensor_tensor(tmp8b[:], iim[:], srs_t[:], OP.mult),
                            waits=[e_], inc="v")
                e_pa = pl.add("ve", lambda e: e.tensor_tensor(pa_t[:], tmp8a[:], tmp8b[:], OP.add),
                              waits=[e_], inc="v")
                e_ = pl.add("ve", lambda e: e.tensor_tensor(tmp8a[:], iim[:], src_t[:], OP.mult),
                            waits=[e_pa], inc="v")
                e_ = pl.add("ve", lambda e: e.tensor_tensor(tmp8b[:], ire[:], srs_t[:], OP.mult),
                            waits=[e_], inc="v")
                e_pb = pl.add("ve", lambda e: e.tensor_tensor(pb_t[:], tmp8a[:], tmp8b[:], OP.subtract),
                              waits=[e_], inc="v")
                e_qa = pl.add("ve", lambda e: e.tensor_tensor(qa_t[:], gte[:], src_t[:], OP.mult),
                              waits=[e_pb], inc="v")
                e_qb = pl.add("ve", lambda e: e.tensor_tensor(qb_t[:], gte[:], srs_t[:], OP.mult),
                              waits=[e_pb], inc="v")
                ev_rotors = [e_pa, e_pb, e_qa, e_qb]
                if upto == "rotors" and si == tap_si:
                    pl.dma("gp", out_d[0:128, 0, 0:NG], pa_t[:], waits=[e_pa], sem="st")
                    pl.dma("gp", out_d[0:128, 0, NG:2 * NG], pb_t[:], waits=[e_pb], sem="st")
                    pl.dma("gp", out_d[0:128, 0, 2 * NG:3 * NG], src_t[:], waits=[e_qb], sem="st")
                    pl.dma("gp", out_d[0:128, 0, 3 * NG:4 * NG], srs_t[:], waits=[e_qb], sem="st")
            ev_xl = []
            for g in range(NG):
                w = [ev_xfree, ev_rot_free[g]]
                ev_xl.append(pl.dma("sy", xseg[g][:],
                                    x_d[g * 128:(g + 1) * 128, b_, t0:t0 + TSEG], waits=w))
            ev_post = []
            for g in range(NG):
                pa = pa_t[:, g:g + 1]
                pb = pb_t[:, g:g + 1]
                qa = qa_t[:, g:g + 1]
                qb = qb_t[:, g:g + 1]
                w0 = [ev_xl[-1], ev_scratch] + ev_rotors
                e_xc = pl.add("ve", lambda e, g=g: e.tensor_tensor(s2[:, 0:TSEG], xseg[g][:], bc[g][:], OP.mult),
                              waits=w0, inc="v")
                e_xs = pl.add("ve", lambda e, g=g: e.tensor_tensor(s3[:, 0:TSEG], xseg[g][:], bs[g][:], OP.mult),
                              waits=w0, inc="v")
                e_t = pl.add("ve", lambda e, pb=pb: e.tensor_scalar(s4[:, 0:TSEG], s3[:, 0:TSEG], pb, None, OP.mult),
                             waits=[e_xs], inc="v")
                wwr = [e_t, ev_rot_free[g]]
                e_wr = pl.add("ve", lambda e, g=g, pa=pa: e.scalar_tensor_tensor(
                    wr[g][:], s2[:, 0:TSEG], pa, s4[:, 0:TSEG], OP.mult, OP.add), waits=wwr, inc="v")
                e_t2 = pl.add("ve", lambda e, pa=pa: e.tensor_scalar(s4[:, 0:TSEG], s3[:, 0:TSEG], pa, None, OP.mult),
                              waits=[e_wr], inc="v")
                e_wi = pl.add("ve", lambda e, g=g, pb=pb: e.scalar_tensor_tensor(
                    wi[g][:], s2[:, 0:TSEG], pb, s4[:, 0:TSEG], OP.mult, OP.subtract), waits=[e_t2], inc="v")
                if upto == "rot0" and si == tap_si and b_ == tap_b:
                    pl.dma("gp", out_d[g * 128:(g + 1) * 128, 0, 0:TSEG], wr[g][:], waits=[e_wr], sem="st")
                    pl.dma("gp", out_d[g * 128:(g + 1) * 128, 0, TSEG:2 * TSEG], wi[g][:], waits=[e_wi], sem="st")
                rho_b = rho[:, g:g + 1].broadcast_to([128, TSEG])
                e_scr = pl.add("ve", lambda e, g=g, rho_b=rho_b, b_=b_: e.tensor_tensor_scan(
                    wr[g][:], rho_b, wr[g][:], vcr[:, b_ * NG + g:b_ * NG + g + 1], OP.mult, OP.add),
                    waits=[e_wr, e_rho, ev_carry[("r", g, b_)]], inc="v")
                e_sci = pl.add("ve", lambda e, g=g, rho_b=rho_b, b_=b_: e.tensor_tensor_scan(
                    wi[g][:], rho_b, wi[g][:], vci[:, b_ * NG + g:b_ * NG + g + 1], OP.mult, OP.add),
                    waits=[e_wi, ev_carry[("i", g, b_)]], inc="v")
                e_cr = pl.add("ve", lambda e, g=g, b_=b_: e.tensor_copy(
                    vcr[:, b_ * NG + g:b_ * NG + g + 1], wr[g][:, TSEG - 1:TSEG]), waits=[e_scr], inc="v")
                e_ci = pl.add("ve", lambda e, g=g, b_=b_: e.tensor_copy(
                    vci[:, b_ * NG + g:b_ * NG + g + 1], wi[g][:, TSEG - 1:TSEG]), waits=[e_sci], inc="v")
                ev_carry[("r", g, b_)] = e_cr
                ev_carry[("i", g, b_)] = e_ci
                if upto == "scan0" and si == tap_si and b_ == tap_b:
                    pl.dma("gp", out_d[g * 128:(g + 1) * 128, 0, 0:TSEG], wr[g][:], waits=[e_scr], sem="st")
                    pl.dma("gp", out_d[g * 128:(g + 1) * 128, 0, TSEG:2 * TSEG], wi[g][:], waits=[e_sci], sem="st")
                e_u1 = pl.add("ve", lambda e, g=g, qa=qa: e.tensor_scalar(
                    s2[:, 0:TSEG], wr[g][:], qa, None, OP.mult), waits=[e_scr], inc="v")
                e_f1 = pl.add("ve", lambda e, g=g, qb=qb: e.scalar_tensor_tensor(
                    s2[:, 0:TSEG], wi[g][:], qb, s2[:, 0:TSEG], OP.mult, OP.subtract),
                    waits=[e_u1, e_sci], inc="v")
                e_u3 = pl.add("ve", lambda e, g=g, qb=qb: e.tensor_scalar(
                    s3[:, 0:TSEG], wr[g][:], qb, None, OP.mult), waits=[e_f1], inc="v")
                e_f2 = pl.add("ve", lambda e, g=g, qa=qa: e.scalar_tensor_tensor(
                    s3[:, 0:TSEG], wi[g][:], qa, s3[:, 0:TSEG], OP.mult, OP.add), waits=[e_u3], inc="v")
                ev_rot_free[g] = e_f2
                e_n1 = pl.add("ve", lambda e, g=g: e.tensor_tensor(
                    s2[:, 0:TSEG], s2[:, 0:TSEG], bc[g][:], OP.mult), waits=[e_f2], inc="v")
                e_n2 = pl.add("ve", lambda e, g=g: e.tensor_tensor(
                    s3[:, 0:TSEG], s3[:, 0:TSEG], bs[g][:], OP.mult), waits=[e_n1], inc="v")
                e_md = pl.add("ve", lambda e: e.tensor_tensor(
                    s2[:, 0:TSEG], s2[:, 0:TSEG], s3[:, 0:TSEG], OP.add), waits=[e_n2], inc="v")
                e_res = pl.add("ve", lambda e, g=g: e.tensor_tensor(
                    xseg[g][:], xseg[g][:], s2[:, 0:TSEG], OP.subtract), waits=[e_md], inc="v")
                if upto == "res0" and si == tap_si and b_ == tap_b:
                    pl.dma("gp", out_d[g * 128:(g + 1) * 128, 0, 0:TSEG], xseg[g][:], waits=[e_res], sem="st")
                ev_post.append(e_res)
            ev_scratch = ev_post[-1]
            if upto in ("rot0", "scan0", "rotors", "res0") and si == tap_si and b_ == tap_b:
                break

            ev_sq = []
            for g in range(NG):
                ev_sq.append(pl.add("sc", lambda e, g=g: e.activation(
                    xb[g][:], xseg[g][:], AF.Square, bias=0.0, scale=1.0),
                    waits=[ev_post[g], ev_xb_free], inc="a"))
            e_mmq = None
            for g in range(NG):
                w = [ev_sq[g], ev_onebc]
                if g == 0 and ev_psQ_free is not None:
                    w.append(ev_psQ_free)
                e_mmq = pl.add("te", lambda e, g=g: e.matmul(
                    psQ[:], onebc[:], xb[g][:], start=(g == 0), stop=(g == NG - 1)),
                    waits=w, inc=("t" if g == NG - 1 else None))
            ev_cpx = []
            for g in range(NG):
                ev_cpx.append(pl.add("ve", lambda e, g=g: e.tensor_copy(xb[g][:], xseg[g][:]),
                                     waits=[e_mmq, ev_post[g]], inc="v"))
            e_mmx = None
            for g in range(NG):
                w = [ev_cpx[g]]
                if g == 0 and ev_psX_free is not None:
                    w.append(ev_psX_free)
                e_mmx = pl.add("te", lambda e, g=g: e.matmul(
                    psX[:], onebc[:], xb[g][:], start=(g == 0), stop=(g == NG - 1)),
                    waits=w, inc=("t" if g == NG - 1 else None))
            e_ = pl.add("ve", lambda e: e.tensor_scalar(stm[:], psX[:], 1.0 / D, None, OP.mult),
                        waits=[e_mmx], inc="v")
            ev_psX_free = e_
            e_ = pl.add("ve", lambda e: e.tensor_scalar(stq[:], psQ[:], 1.0 / D, None, OP.mult),
                        waits=[e_, e_mmq], inc="v")
            ev_psQ_free = e_
            e_ = pl.add("ve", lambda e: e.tensor_tensor(stv[:], stm[:], stm[:], OP.mult),
                        waits=[e_], inc="v")
            e_var = pl.add("ve", lambda e: e.tensor_tensor(stv[:], stq[:], stv[:], OP.subtract),
                           waits=[e_], inc="v")
            e_srt = pl.add("sc", lambda e: e.activation(stv[:], stv[:], AF.Sqrt,
                                                        bias=cst[0:1, C_LNEPS:C_LNEPS + 1], scale=1.0),
                           waits=[e_var], inc="a")
            e_rstd = pl.add("ve", lambda e: e.reciprocal(str_[:], stv[:]), waits=[e_srt], inc="v")
            e_cb = pl.add("ve", lambda e: e.tensor_tensor(stc[:], stm[:], str_[:], OP.mult),
                          waits=[e_rstd], inc="v")
            e_cb = pl.add("ve", lambda e: e.tensor_scalar(stc[:], stc[:], -1.0, None, OP.mult),
                          waits=[e_cb], inc="v")
            wbc = [e_rstd, e_cb, ev_small]
            if ev_psBC_free is not None:
                wbc.append(ev_psBC_free)
            e_bca = pl.add("te", lambda e: e.matmul(psBC[:, 0:512], oner[0:1, :], str_[:],
                                                    start=True, stop=True), waits=wbc, inc="t")
            e_bcb = pl.add("te", lambda e: e.matmul(psBC[:, 512:1024], oner[0:1, :], stc[:],
                                                    start=True, stop=True), waits=[e_bca], inc="t")
            ev_ln = []
            for g in range(NG):
                e_ = pl.add("ve", lambda e, g=g: e.tensor_tensor(
                    s2[:, 0:TSEG], xseg[g][:], psBC[:, 0:512], OP.mult),
                    waits=[e_bcb, ev_cpx[g]], inc="v")
                e_ = pl.add("ve", lambda e, g=g: e.tensor_tensor(
                    xseg[g][:], s2[:, 0:TSEG], psBC[:, 512:1024], OP.add), waits=[e_], inc="v")
                e_ = pl.add("ve", lambda e, g=g: e.tensor_copy(xb[g][:], xseg[g][:]), waits=[e_], inc="v")
                ev_ln.append(e_)
            ev_psBC_free = ev_ln[-1]

            if upto == "ln1":
                last_store = None
                for g in range(NG):
                    last_store = pl.dma("sy", out_d[g * 128:(g + 1) * 128, b_, t0:t0 + TSEG],
                                        xseg[g][:], waits=[ev_ln[g]], sem="st")
                ev_xfree = last_store
                ev_xb_free = ev_ln[-1]
                if b_ == BPC - 1 and si < NSEG_PER_B - 1:
                    wsr = ev_rot_done + ev_sr + ev_rotors
                    if ev_sr_prep is not None:
                        wsr.append(ev_sr_prep)
                    e_ = pl.add("ve", lambda e: e.tensor_tensor(tmp8a[:], src_t[:], rc[512][:], OP.mult),
                                waits=wsr, inc="v")
                    e_ = pl.add("ve", lambda e: e.tensor_tensor(tmp8b[:], srs_t[:], rs[512][:], OP.mult),
                                waits=[e_], inc="v")
                    e_nc2 = pl.add("ve", lambda e: e.tensor_tensor(tmp8a[:], tmp8a[:], tmp8b[:], OP.subtract),
                                   waits=[e_], inc="v")
                    e_ = pl.add("ve", lambda e: e.tensor_tensor(tmp8b[:], src_t[:], rs[512][:], OP.mult),
                                waits=[e_nc2], inc="v")
                    e_ns2 = pl.add("ve", lambda e: e.tensor_tensor(s4[:, 0:NG], srs_t[:], rc[512][:], OP.mult),
                                   waits=[e_], inc="v")
                    e_s2u = pl.add("ve", lambda e: e.tensor_tensor(srs_t[:], s4[:, 0:NG], tmp8b[:], OP.add),
                                   waits=[e_ns2], inc="v")
                    e_c2u = pl.add("ve", lambda e: e.tensor_copy(src_t[:], tmp8a[:]), waits=[e_s2u], inc="v")
                    ev_sr_prep = e_c2u
                continue

            last_fin = None
            last_store = None
            for f in range(NSUB):
                c0, c1c = f * FS, (f + 1) * FS
                ev_q = None
                for cg in range(NCG):
                    e_mm = None
                    for j in range(CGN):
                        hc = cg * CGN + j
                        for kg in range(NG):
                            w = [ev_ln[kg], ev_w1]
                            if ev_psA_free is not None and j == 0 and kg == 0:
                                w.append(ev_psA_free)
                            e_mm = pl.add("te", lambda e, j=j, hc=hc, kg=kg, c0=c0, c1c=c1c: e.matmul(
                                psA[:, j * FS:(j + 1) * FS], w1b[kg][:, hc * 128:(hc + 1) * 128],
                                xb[kg][:, c0:c1c], start=(kg == 0), stop=(kg == NG - 1)),
                                waits=w, inc=("t" if (kg == NG - 1 and j == CGN - 1) else None))
                    ev_ev = []
                    for j in range(CGN):
                        hc = cg * CGN + j
                        w = [e_mm]
                        if ev_q is not None:
                            w.append(ev_q)
                        elif ev_h4_free is not None:
                            w.append(ev_h4_free)
                        ev_ev.append(pl.add("sc", lambda e, j=j, hc=hc: e.activation(
                            h4[:, j * FS:(j + 1) * FS], psA[:, j * FS:(j + 1) * FS], AF.Relu,
                            bias=b1c[:, hc:hc + 1], scale=1.0), waits=w, inc="a"))
                    ev_psA_free = ev_ev[-1]
                    w_hq = [ev_hq_free] if (ev_hq_free is not None and cg == 0) else []
                    w_h4 = [ev_h4_free] if (ev_h4_free is not None and ev_q is None) else []
                    ready = pl.nop("ve", [ev_ev[-1]] + w_hq + w_h4)
                    ev_q = emit_quant(pl, h4[:, 0:1024], hq[:, cg * 1024:(cg + 1) * 1024], 1024,
                                      False, True, cst, kt, s2, s3, s4, ready)
                ev_h4_free = ev_q
                ev_ev2 = []
                for wv in range(2):
                    e_mm2 = None
                    for dj in range(4):
                        dg = wv * 4 + dj
                        for hc in range(NHC):
                            w = [ev_q, ev_w2]
                            if ev_psB_free is not None and dj == 0 and hc == 0:
                                w.append(ev_psB_free)
                            e_mm2 = pl.add("te", lambda e, dj=dj, dg=dg, hc=hc: e.matmul(
                                psB[:, dj * FS:(dj + 1) * FS], w2q[hc][:, dg * 128:(dg + 1) * 128],
                                hq[:, hc * FS:(hc + 1) * FS], start=(hc == 0), stop=(hc == NHC - 1)),
                                waits=w, inc=("t" if (hc == NHC - 1 and dj == 3) else None))
                    for dj in range(4):
                        dg = wv * 4 + dj
                        w = [e_mm2, ev_bq]
                        if ev_oq_free is not None and wv == 0 and dj == 0:
                            w.append(ev_oq_free)
                        ev_ev2.append(pl.add("sc", lambda e, dj=dj, dg=dg: e.activation(
                            oq[:, dg * FS:(dg + 1) * FS], psB[:, dj * FS:(dj + 1) * FS], AF.Identity,
                            bias=bqt[:, dg:dg + 1], scale=1.0), waits=w, inc="a"))
                    ev_psB_free = ev_ev2[-1]
                ev_hq_free = ev_ev2[-1]
                ev_oqq = []
                for sl_i in range(2):
                    ready = pl.nop("ve", ev_ev2[4 * sl_i:4 * sl_i + 4])
                    ev_oqq.append(emit_quant(pl, oq[:, sl_i * 1024:(sl_i + 1) * 1024],
                                             oq[:, sl_i * 1024:(sl_i + 1) * 1024], 1024,
                                             True, False, cst, kt, s2, s3, s4, ready))
                ev_r2 = []
                for g in range(NG):
                    ev_r2.append(pl.add("ve", lambda e, g=g, c0=c0, c1c=c1c: e.tensor_tensor(
                        oq[:, g * FS:(g + 1) * FS], xseg[g][:, c0:c1c], oq[:, g * FS:(g + 1) * FS], OP.add),
                        waits=[ev_oqq[g // 4], ev_ln[g]], inc="v"))
                e_mmq2 = None
                e_mmx2 = None
                for g in range(NG):
                    half = s2 if (g % 2 == 0) else s3
                    e_sq2 = pl.add("sc", lambda e, g=g, half=half: e.activation(
                        half[:, 0:FS], oq[:, g * FS:(g + 1) * FS], AF.Square, bias=0.0, scale=1.0),
                        waits=[ev_r2[g]] + ([e_mmq2] if e_mmq2 else [ev_r2[-1]]), inc="a")
                    wq2 = [e_sq2]
                    wx2 = [ev_r2[g]]
                    if g == 0:
                        if ev_psQ_free is not None:
                            wq2.append(ev_psQ_free)
                        if ev_psX_free is not None:
                            wx2.append(ev_psX_free)
                    e_mmq2 = pl.add("te", lambda e, g=g, half=half: e.matmul(
                        psQ[:, 0:FS], onec[:], half[:, 0:FS], start=(g == 0), stop=(g == NG - 1)),
                        waits=wq2, inc="t")
                    e_mmx2 = pl.add("te", lambda e, g=g: e.matmul(
                        psX[:, 0:FS], onec[:], oq[:, g * FS:(g + 1) * FS], start=(g == 0), stop=(g == NG - 1)),
                        waits=wx2, inc=("t" if g == NG - 1 else None))
                e_ = pl.add("ve", lambda e: e.tensor_scalar(stm[:, 0:FS], psX[:, 0:FS], 1.0 / D, None, OP.mult),
                            waits=[e_mmx2], inc="v")
                ev_psX_free = e_
                e_ = pl.add("ve", lambda e: e.tensor_scalar(stq[:, 0:FS], psQ[:, 0:FS], 1.0 / D, None, OP.mult),
                            waits=[e_, e_mmq2], inc="v")
                ev_psQ_free = e_
                e_ = pl.add("ve", lambda e: e.tensor_tensor(stv[:, 0:FS], stm[:, 0:FS], stm[:, 0:FS], OP.mult),
                            waits=[e_], inc="v")
                e_var2 = pl.add("ve", lambda e: e.tensor_tensor(stv[:, 0:FS], stq[:, 0:FS], stv[:, 0:FS], OP.subtract),
                                waits=[e_], inc="v")
                e_srt2 = pl.add("sc", lambda e: e.activation(stv[:, 0:FS], stv[:, 0:FS], AF.Sqrt,
                                                             bias=cst[0:1, C_LNEPS:C_LNEPS + 1], scale=1.0),
                                waits=[e_var2], inc="a")
                e_rstd2 = pl.add("ve", lambda e: e.reciprocal(str_[:, 0:FS], stv[:, 0:FS]),
                                 waits=[e_srt2], inc="v")
                e_cb2 = pl.add("ve", lambda e: e.tensor_tensor(stc[:, 0:FS], stm[:, 0:FS], str_[:, 0:FS], OP.mult),
                               waits=[e_rstd2], inc="v")
                e_cb2 = pl.add("ve", lambda e: e.tensor_scalar(stc[:, 0:FS], stc[:, 0:FS], -1.0, None, OP.mult),
                               waits=[e_cb2], inc="v")
                e_bca2 = pl.add("te", lambda e: e.matmul(psBC[:, 0:FS], oner[0:1, :], str_[:, 0:FS],
                                                         start=True, stop=True),
                                waits=[e_rstd2, e_cb2, ev_psBC_free], inc="t")
                e_bcb2 = pl.add("te", lambda e: e.matmul(psBC[:, 512:512 + FS], oner[0:1, :], stc[:, 0:FS],
                                                         start=True, stop=True), waits=[e_bca2], inc="t")
                ev_fin = []
                for g in range(NG):
                    e_ = pl.add("ve", lambda e, g=g: e.tensor_tensor(
                        s4[:, 0:FS], oq[:, g * FS:(g + 1) * FS], psBC[:, 0:FS], OP.mult),
                        waits=[e_bcb2], inc="v")
                    e_ = pl.add("ve", lambda e, g=g: e.tensor_tensor(
                        oq[:, g * FS:(g + 1) * FS], s4[:, 0:FS], psBC[:, 512:512 + FS], OP.add),
                        waits=[e_], inc="v")
                    ev_fin.append(e_)
                ev_psBC_free = ev_fin[-1]
                for g in range(NG):
                    last_store = pl.dma("sy", out_d[g * 128:(g + 1) * 128, b_, t0 + c0:t0 + c1c],
                                        oq[:, g * FS:(g + 1) * FS], waits=[ev_fin[g]], sem="st")
                ev_oq_free = last_store
                last_fin = ev_fin[-1]
            ev_xfree = last_fin
            ev_xb_free = last_fin

            if b_ == BPC - 1 and si < NSEG_PER_B - 1:
                wsr = ev_rot_done + ev_sr + ev_rotors
                if ev_sr_prep is not None:
                    wsr.append(ev_sr_prep)
                e_ = pl.add("ve", lambda e: e.tensor_tensor(tmp8a[:], src_t[:], rc[512][:], OP.mult),
                            waits=wsr, inc="v")
                e_ = pl.add("ve", lambda e: e.tensor_tensor(tmp8b[:], srs_t[:], rs[512][:], OP.mult),
                            waits=[e_], inc="v")
                e_nc2 = pl.add("ve", lambda e: e.tensor_tensor(tmp8a[:], tmp8a[:], tmp8b[:], OP.subtract),
                               waits=[e_], inc="v")
                e_ = pl.add("ve", lambda e: e.tensor_tensor(tmp8b[:], src_t[:], rs[512][:], OP.mult),
                            waits=[e_nc2], inc="v")
                e_ns2 = pl.add("ve", lambda e: e.tensor_tensor(s4[:, 0:NG], srs_t[:], rc[512][:], OP.mult),
                               waits=[e_], inc="v")
                e_s2u = pl.add("ve", lambda e: e.tensor_tensor(srs_t[:], s4[:, 0:NG], tmp8b[:], OP.add),
                               waits=[e_ns2], inc="v")
                e_c2u = pl.add("ve", lambda e: e.tensor_copy(src_t[:], tmp8a[:]), waits=[e_s2u], inc="v")
                ev_sr_prep = e_c2u

        total_store = pl.cnt["st"]
        pl.add("sy", None, waits=[("st", total_store)])

        def run_queue(eng_obj, name):
            hw = {s: 0 for s in SEMS}
            for (fn, waits, inc, n) in pl.q[name]:
                for (s, val) in waits:
                    if val > hw[s]:
                        eng_obj.wait_ge(sem[s], val)
                        hw[s] = val
                if fn is None:
                    continue
                r = fn(eng_obj)
                if inc is not None:
                    r.then_inc(sem[inc], n)

        @block.sync
        def _(sync):
            run_queue(sync, "sy")

        @block.vector
        def _(vector):
            run_queue(vector, "ve")

        @block.scalar
        def _(scalar):
            run_queue(scalar, "sc")

        @block.gpsimd
        def _(gpsimd):
            run_queue(gpsimd, "gp")

        @block.tensor
        def _(tensor):
            run_queue(tensor, "te")

    return nc


_NC_CACHE = None


def kernel(L0_attn_x, L0_attn_phazor_init_re, L0_attn_phazor_init_im,
           L0_attn_phazor_angle, L0_attn_ln_minus_ln_phazor_abs,
           L0_attn_last_conv_re, L0_attn_last_conv_im, L0_attn_ln_phazor_scale,
           L0_ffn_w1, L0_ffn_b1, L0_ffn_w2, L0_ffn_b2):
    global _NC_CACHE
    x = np.ascontiguousarray(np.asarray(L0_attn_x, dtype=np.float32))
    w1 = np.ascontiguousarray(np.asarray(L0_ffn_w1, dtype=np.float32))
    b1 = np.ascontiguousarray(np.asarray(L0_ffn_b1, dtype=np.float32))
    w2 = np.ascontiguousarray(np.asarray(L0_ffn_w2, dtype=np.float32))
    b2 = np.ascontiguousarray(np.asarray(L0_ffn_b2, dtype=np.float32))

    def chan(v):
        return np.ascontiguousarray(np.asarray(v, dtype=np.float32).reshape(NG, 128).T)

    w1t = np.ascontiguousarray(w1.T)
    w2t = np.ascontiguousarray(w2.T)
    b1c = np.ascontiguousarray(b1.reshape(NHC, 128).T)
    b2c = chan(b2)
    cstv = np.ascontiguousarray(np.broadcast_to(CONST_VALUES, (128, NCONST)))
    io8 = np.ascontiguousarray(np.broadcast_to(np.arange(8, dtype=np.float32), (128, 8)))
    onec = np.ones((128, 1), dtype=np.float32)
    oner = np.ones((1, 128), dtype=np.float32)

    shared = dict(w1t=w1t, w2t=w2t, b1c=b1c, b2c=b2c,
                  angle=chan(L0_attn_phazor_angle), lnml=chan(L0_attn_ln_minus_ln_phazor_abs),
                  scl=chan(L0_attn_ln_phazor_scale), ire=chan(L0_attn_phazor_init_re),
                  iim=chan(L0_attn_phazor_init_im), lre=chan(L0_attn_last_conv_re),
                  lim=chan(L0_attn_last_conv_im), cst=cstv, iota8=io8, onec=onec, oner=oner)

    in_maps = []
    for c in range(NCORES):
        xc = np.ascontiguousarray(x[c * BPC:(c + 1) * BPC].transpose(2, 0, 1))
        m = dict(shared)
        m["x"] = xc
        in_maps.append(m)

    if _NC_CACHE is None:
        _NC_CACHE = build_nc()
    res = run_bass_kernel_spmd(_NC_CACHE, in_maps, core_ids=list(range(NCORES)))
    out = np.empty((B, L, D), dtype=np.float32)
    for c in range(NCORES):
        oc = np.asarray(res.results[c]["out"])
        out[c * BPC:(c + 1) * BPC] = oc.transpose(1, 2, 0)
    return out


# version 04
# speedup vs baseline: 1.0021x; 1.0021x over previous
import numpy as np
from contextlib import ExitStack

import concourse.bass as bass
import concourse.mybir as mybir
from concourse.bass_utils import run_bass_kernel_spmd

F32 = mybir.dt.float32
BF16 = mybir.dt.bfloat16
FP8 = mybir.dt.float8e4
AF = mybir.ActivationFunctionType
OP = mybir.AluOpType

B, L, D, H = 16, 4096, 1024, 4096
NCORES = 8
BPC = B // NCORES
NG = D // 128
NHC = H // 128
TSEG = 512
NSEG_PER_B = L // TSEG
FS = 256
NSUB = TSEG // FS
CGN = 4
NCG = NHC // CGN

LN2 = float(np.log(2.0))
INV_LN2 = 1.0 / LN2
M32 = 1.5 * 2.0 ** 23
M64 = 1.5 * 2.0 ** 52
PIH = float(np.pi / 2.0)

C_LNBIAS_H = 0
C_NEGLN2 = 1
C_POSLN2 = 2
C_9LN2 = 3
C_N9LN2 = 4
C_24LN2 = 5
C_N24LN2 = 6
C_LNBIAS_O = 7
C_PIH = 8
C_NEG1 = 9
C_LNEPS = 10
NCONST = 16

CONST_VALUES = np.zeros(NCONST, dtype=np.float32)
CONST_VALUES[C_LNBIAS_H] = 2.0 ** -45
CONST_VALUES[C_NEGLN2] = -LN2
CONST_VALUES[C_POSLN2] = LN2
CONST_VALUES[C_9LN2] = 9.0 * LN2
CONST_VALUES[C_N9LN2] = -9.0 * LN2
CONST_VALUES[C_24LN2] = 24.0 * LN2
CONST_VALUES[C_N24LN2] = -24.0 * LN2
CONST_VALUES[C_LNBIAS_O] = 2.0 ** -60
CONST_VALUES[C_PIH] = PIH
CONST_VALUES[C_NEG1] = -1.0
CONST_VALUES[C_LNEPS] = 1e-5

ENGS = ("sy", "ve", "sc", "gp", "te")
SEMS = ("ld", "st", "v", "a", "g", "t")
SERIAL = False
ENG_SEM = {"ve": "v", "sc": "a", "gp": "g", "te": "t"}


class Plan:

    def __init__(self):
        self.q = {e: [] for e in ENGS}
        self.cnt = {s: 0 for s in SEMS}
        self.region = "startup"

    def add(self, eng, fn, waits=(), inc=None, n=1):
        region_ok = True
        if isinstance(SERIAL, (set, list)):
            regs = [x.split("@")[1] for x in SERIAL if "@" in x]
            if regs:
                region_ok = self.region in regs
        if region_ok and SERIAL and eng in ENG_SEM and (SERIAL is True or any(x.split("@")[0].startswith(eng) for x in (SERIAL if isinstance(SERIAL, (set, list)) else []))):
            sel = None
            if isinstance(SERIAL, (set, list)):
                for x in SERIAL:
                    x0 = x.split("@")[0]
                    if x0.startswith(eng):
                        sel = x0[len(eng):]
            semset = ("v", "a", "t", "g") if (SERIAL is True or not sel) else tuple(sel)
            waits = list(waits) + [(s, self.cnt[s]) for s in semset if s != ENG_SEM.get(eng)]
            if inc is None:
                inc = ENG_SEM[eng]
        w = {}
        for ev in waits:
            if ev is None:
                continue
            if isinstance(ev, list):
                for e2 in ev:
                    if e2 is not None:
                        s, val = e2
                        w[s] = max(w.get(s, 0), val)
                continue
            s, val = ev
            w[s] = max(w.get(s, 0), val)
        ev_out = None
        if inc is not None:
            self.cnt[inc] += n
            ev_out = (inc, self.cnt[inc])
        self.q[eng].append((fn, sorted(w.items()), inc, n))
        return ev_out

    def dma(self, eng, out_ap, in_ap, waits=(), sem="ld"):
        return self.add(eng, lambda e, o=out_ap, i=in_ap: e.dma_start(out=o, in_=i),
                        waits=waits, inc=sem, n=16)

    def nop(self, eng, waits):
        sem = {"ve": "v", "sc": "a", "gp": "g", "te": "t"}[eng]
        return self.add(eng, lambda e: e.engine_nop(), waits=waits, inc=sem)


def emit_quant(pl, src, dst, width, signed, e4m3, cst, kt, s2, s3, s4, ready_ev):
    wshift = 6.0 if e4m3 else 14.0
    c1 = INV_LN2 * (0.5 if signed else 1.0)
    ci_p = C_9LN2 if e4m3 else C_24LN2
    ci_n = C_N9LN2 if e4m3 else C_N24LN2
    sl = lambda t: t[:, 0:width]
    if signed:
        e_sq = pl.add("sc", lambda e: e.activation(sl(s2), src, AF.Square, bias=0.0, scale=1.0),
                      waits=[ready_ev], inc="a")
        e_ln = pl.add("sc", lambda e: e.activation(sl(s2), sl(s2), AF.Ln,
                                                   bias=cst[:, C_LNBIAS_O:C_LNBIAS_O + 1], scale=1.0),
                      waits=[e_sq], inc="a")
    else:
        e_ln = pl.add("sc", lambda e: e.activation(sl(s2), src, AF.Ln,
                                                   bias=cst[:, C_LNBIAS_H:C_LNBIAS_H + 1], scale=1.0),
                      waits=[ready_ev], inc="a")
    e_t1 = pl.add("ve", lambda e: e.tensor_scalar(sl(s3), sl(s2), c1, wshift - 0.5, OP.mult, OP.add),
                  waits=[e_ln, ready_ev], inc="v")
    e_A1 = pl.add("ve", lambda e: e.tensor_scalar(sl(s4), sl(s3), M32, M32, OP.add, OP.max), waits=[], inc="v")
    e_A = pl.add("ve", lambda e: e.tensor_scalar(sl(s4), sl(s4), M32, None, OP.subtract), waits=[], inc="v")
    e_D1 = pl.add("ve", lambda e: e.tensor_scalar(sl(s3), sl(s3), M64, M64, OP.add, OP.max), waits=[], inc="v")
    e_D = pl.add("ve", lambda e: e.tensor_scalar(sl(s3), sl(s3), M64, None, OP.subtract), waits=[], inc="v")
    e_fl = pl.add("ve", lambda e: e.scalar_tensor_tensor(sl(s2), sl(s4), kt, sl(s3), OP.mult, OP.add),
                  waits=[], inc="v")
    e_si = pl.add("sc", lambda e: e.activation(sl(s3), sl(s2), AF.Exp,
                                               bias=cst[:, ci_p:ci_p + 1], scale=cst[:, C_NEGLN2:C_NEGLN2 + 1]),
                  waits=[e_fl], inc="a")
    e_s = pl.add("sc", lambda e: e.activation(sl(s4), sl(s2), AF.Exp,
                                              bias=cst[:, ci_n:ci_n + 1], scale=cst[:, C_POSLN2:C_POSLN2 + 1]),
                 waits=[e_fl], inc="a")
    e_y = pl.add("ve", lambda e: e.tensor_tensor(sl(s2), src, sl(s3), OP.mult),
                 waits=[e_si, e_s], inc="v")
    e_rA = pl.add("ve", lambda e: e.tensor_scalar(sl(s3), sl(s2), M32, M32, OP.add, OP.subtract),
                  waits=[], inc="v")
    e_rD = pl.add("ve", lambda e: e.tensor_scalar(sl(s2), sl(s2), M64, M64, OP.add, OP.subtract),
                  waits=[], inc="v")
    e_r = pl.add("ve", lambda e: e.scalar_tensor_tensor(sl(s3), sl(s3), kt, sl(s2), OP.mult, OP.add),
                 waits=[], inc="v")
    e_q = pl.add("ve", lambda e: e.tensor_tensor(dst, sl(s3), sl(s4), OP.mult),
                 waits=[e_s], inc="v")
    return e_q


def build_nc(upto=None, tap_si=0, tap_b=0):
    nc = bass.Bass("TRN2", target_bir_lowering=False, detect_race_conditions=False)

    x_d = nc.dram_tensor("x", [D, BPC, L], F32, kind="ExternalInput").ap()
    w1t_d = nc.dram_tensor("w1t", [D, H], F32, kind="ExternalInput").ap()
    w2t_d = nc.dram_tensor("w2t", [H, D], F32, kind="ExternalInput").ap()
    b1c_d = nc.dram_tensor("b1c", [128, NHC], F32, kind="ExternalInput").ap()
    b2c_d = nc.dram_tensor("b2c", [128, NG], F32, kind="ExternalInput").ap()
    ang_d = nc.dram_tensor("angle", [128, NG], F32, kind="ExternalInput").ap()
    lnm_d = nc.dram_tensor("lnml", [128, NG], F32, kind="ExternalInput").ap()
    scl_d = nc.dram_tensor("scl", [128, NG], F32, kind="ExternalInput").ap()
    ire_d = nc.dram_tensor("ire", [128, NG], F32, kind="ExternalInput").ap()
    iim_d = nc.dram_tensor("iim", [128, NG], F32, kind="ExternalInput").ap()
    lre_d = nc.dram_tensor("lre", [128, NG], F32, kind="ExternalInput").ap()
    lim_d = nc.dram_tensor("lim", [128, NG], F32, kind="ExternalInput").ap()
    cst_d = nc.dram_tensor("cst", [128, NCONST], F32, kind="ExternalInput").ap()
    io8_d = nc.dram_tensor("iota8", [128, 8], F32, kind="ExternalInput").ap()
    one_d = nc.dram_tensor("onec", [128, 1], F32, kind="ExternalInput").ap()
    oner_d = nc.dram_tensor("oner", [1, 128], F32, kind="ExternalInput").ap()
    out_d = nc.dram_tensor("out", [D, BPC, L], F32, kind="ExternalOutput").ap()

    with ExitStack() as es:
        sb = lambda name, shape, dt: es.enter_context(nc.sbuf_tensor(name, shape, dt))
        psal = lambda name, shape: es.enter_context(nc.psum_tensor(name, shape, F32))

        w1b = [sb(f"w1b{g}", [128, H], BF16) for g in range(NG)]
        w2q = [sb(f"w2q{c}", [128, D], FP8) for c in range(NHC)]
        xseg = [sb(f"xseg{g}", [128, TSEG], F32) for g in range(NG)]
        wr = [sb(f"wr{g}", [128, TSEG], BF16) for g in range(NG)]
        wi = [sb(f"wi{g}", [128, TSEG], BF16) for g in range(NG)]
        bc = [sb(f"bc{g}", [128, TSEG], BF16) for g in range(NG)]
        bs = [sb(f"bs{g}", [128, TSEG], BF16) for g in range(NG)]
        xb = [sb(f"xb{g}", [128, TSEG], BF16) for g in range(NG)]
        h4 = sb("h4", [128, CGN * FS], F32)
        hq = sb("hq", [128, NHC * FS], FP8)
        oq = sb("oq", [128, NG * FS], F32)
        s1 = sb("s1", [128, 1024], F32)
        s2 = sb("s2", [128, 1024], F32)
        s3 = sb("s3", [128, 1024], F32)
        s4 = sb("s4", [128, 1024], F32)

        b1c = sb("b1cs", [128, NHC], F32)
        b2c = sb("b2cs", [128, NG], F32)
        bqt = sb("bqt", [128, NG], F32)
        ang = sb("angs", [128, NG], F32)
        lnm = sb("lnms", [128, NG], F32)
        scl = sb("scls", [128, NG], F32)
        ire = sb("ires", [128, NG], F32)
        iim = sb("iims", [128, NG], F32)
        lre = sb("lres", [128, NG], F32)
        lim = sb("lims", [128, NG], F32)
        cst = sb("csts", [128, NCONST], F32)
        io8 = sb("io8s", [128, 8], F32)
        onec = sb("onecs", [128, 1], F32)
        oner = sb("oners", [1, 128], F32)
        kt_t = sb("kt", [128, 1], F32)
        onebc = sb("onebc", [128, 1], BF16)

        dec = sb("dec", [128, NG], F32)
        rho = sb("rho", [128, NG], F32)
        gte = sb("gte", [128, NG], F32)
        phi = sb("phi", [128, NG], F32)
        rc = {n: sb(f"rc{n}", [128, NG], F32) for n in (8, 16, 32, 64, 128, 256, 512)}
        rs = {n: sb(f"rs{n}", [128, NG], F32) for n in (8, 16, 32, 64, 128, 256, 512)}
        src_t = sb("srcos", [128, NG], F32)
        srs_t = sb("srsin", [128, NG], F32)
        pa_t = sb("pa", [128, NG], F32)
        pb_t = sb("pb", [128, NG], F32)
        qa_t = sb("qa", [128, NG], F32)
        qb_t = sb("qb", [128, NG], F32)
        tmp8a = sb("tmp8a", [128, NG], F32)
        tmp8b = sb("tmp8b", [128, NG], F32)
        vsp = sb("vsp", [128, NG], F32)
        vcp = sb("vcp", [128, NG], F32)
        vcr = sb("vcr", [128, NG * BPC], F32)
        vci = sb("vci", [128, NG * BPC], F32)
        stm = sb("stm", [1, TSEG], F32)
        stq = sb("stq", [1, TSEG], F32)
        stv = sb("stv", [1, TSEG], F32)
        str_ = sb("strs", [1, TSEG], F32)
        stc = sb("stc", [1, TSEG], F32)

        psA = psal("psA", [128, CGN * FS])
        psB = psal("psB", [128, 4 * FS])
        psBC = psal("psBC", [128, 1024])
        psX = psal("psX", [1, TSEG])
        psQ = psal("psQ", [1, TSEG])

        sem = {s: es.enter_context(nc.semaphore(s)) for s in SEMS}
        block = es.enter_context(nc.Block())

        pl = Plan()
        kt = kt_t[:, 0:1]

        ev_small = None
        for t_sb, t_d in [(b1c, b1c_d), (b2c, b2c_d), (ang, ang_d), (lnm, lnm_d),
                          (scl, scl_d), (ire, ire_d), (iim, iim_d), (lre, lre_d),
                          (lim, lim_d), (io8, io8_d), (onec, one_d), (oner, oner_d),
                          (cst, cst_d)]:
            ev_small = pl.dma("sy", t_sb[:], t_d)

        e = pl.add("ve", lambda e: e.tensor_scalar(kt_t[:], onec[:], 2.0 ** -30, None, OP.add),
                   waits=[ev_small], inc="v")
        e = pl.add("ve", lambda e: e.tensor_scalar(kt_t[:], kt_t[:], 1.0, -(2.0 ** 30), OP.subtract, OP.mult),
                   waits=[e], inc="v")
        ev_kt = pl.add("ve", lambda e: e.tensor_scalar(kt_t[:], kt_t[:], 1.0, None, OP.add),
                       waits=[e], inc="v")
        ev_onebc = pl.add("ve", lambda e: e.tensor_copy(onebc[:], onec[:]), waits=[ev_small], inc="v")

        e_dec = pl.add("sc", lambda e: e.activation(dec[:], lnm[:], AF.Exp, bias=0.0, scale=1.0),
                       waits=[ev_small], inc="a")
        e_dsc = pl.add("ve", lambda e: e.tensor_tensor(tmp8a[:], dec[:], scl[:], OP.mult),
                       waits=[e_dec], inc="v")
        e_rho = pl.add("sc", lambda e: e.activation(rho[:], tmp8a[:], AF.Exp, bias=0.0,
                                                    scale=cst[:, C_NEG1:C_NEG1 + 1]),
                       waits=[e_dsc], inc="a")
        e_gte = pl.add("ve", lambda e: e.tensor_scalar(gte[:], rho[:], -1.0, 1.0, OP.mult, OP.add),
                       waits=[e_rho], inc="v")
        e_phi = pl.add("ve", lambda e: e.tensor_tensor(phi[:], ang[:], scl[:], OP.mult),
                       waits=[ev_small], inc="v")
        e1 = pl.add("ve", lambda e: e.tensor_scalar(tmp8b[:], phi[:], 8.0, None, OP.mult),
                    waits=[e_phi, e_dsc], inc="v")
        e_s8 = pl.add("sc", lambda e: e.activation(rs[8][:], tmp8b[:], AF.Sin, bias=0.0, scale=1.0),
                      waits=[e1], inc="a")
        e_c8 = pl.add("sc", lambda e: e.activation(rc[8][:], tmp8b[:], AF.Sin,
                                                   bias=cst[:, C_PIH:C_PIH + 1], scale=1.0),
                      waits=[e1], inc="a")
        prev_c, prev_s = rc[8], rs[8]
        last_sq = [e_s8, e_c8, e_gte]
        for n in (16, 32, 64, 128, 256, 512):
            pc, ps_ = prev_c, prev_s
            ecs = pl.add("ve", lambda e, pc=pc, ps_=ps_: e.tensor_tensor(tmp8a[:], pc[:], ps_[:], OP.mult),
                         waits=last_sq, inc="v")
            ecc = pl.add("ve", lambda e, pc=pc: e.tensor_tensor(tmp8b[:], pc[:], pc[:], OP.mult),
                         waits=last_sq, inc="v")
            ec2 = pl.add("ve", lambda e, n=n: e.tensor_scalar(rc[n][:], tmp8b[:], 2.0, 1.0, OP.mult, OP.subtract),
                         waits=[ecc], inc="v")
            es2 = pl.add("ve", lambda e, n=n: e.tensor_scalar(rs[n][:], tmp8a[:], 2.0, None, OP.mult),
                         waits=[ecs], inc="v")
            last_sq = [ec2, es2]
            prev_c, prev_s = rc[n], rs[n]
        ev_rot_done = last_sq

        e_srcontainerc = pl.add("ve", lambda e: e.tensor_scalar(src_t[:], ang[:], 0.0, 1.0, OP.mult, OP.add),
                                waits=[ev_small], inc="v")
        e_srs0 = pl.add("ve", lambda e: e.tensor_scalar(srs_t[:], ang[:], 0.0, None, OP.mult),
                        waits=[ev_small], inc="v")
        ev_sr = [e_srcontainerc] if False else [e_srcontainerc, e_srs0]

        e_sp = pl.add("sc", lambda e: e.activation(vsp[:], phi[:], AF.Sin, bias=0.0, scale=1.0),
                      waits=[e_phi, e_c8], inc="a")
        e_cp = pl.add("sc", lambda e: e.activation(vcp[:], phi[:], AF.Sin,
                                                   bias=cst[:, C_PIH:C_PIH + 1], scale=1.0),
                      waits=[e_phi, e1, e_sp], inc="a")
        sA = s1[:, 0:NG]
        sB = s1[:, NG:2 * NG]
        sC = s1[:, 2 * NG:3 * NG]
        sD = s1[:, 3 * NG:4 * NG]
        e_a = pl.add("ve", lambda e: e.tensor_tensor(sA, vcp[:], lre[:], OP.mult),
                     waits=[e_cp, ev_small], inc="v")
        e_b = pl.add("ve", lambda e: e.tensor_tensor(sB, vsp[:], lim[:], OP.mult),
                     waits=[e_cp], inc="v")
        e_c = pl.add("ve", lambda e: e.tensor_tensor(sC, vsp[:], lre[:], OP.mult),
                     waits=[e_cp], inc="v")
        e_d = pl.add("ve", lambda e: e.tensor_tensor(sD, vcp[:], lim[:], OP.mult),
                     waits=[e_cp], inc="v")
        ev_vinit = []
        for b_ in range(BPC):
            evr = pl.add("ve", lambda e, b_=b_: e.tensor_tensor(
                vcr[:, b_ * NG:(b_ + 1) * NG], sA, sB, OP.subtract), waits=[e_a, e_b], inc="v")
            evi = pl.add("ve", lambda e, b_=b_: e.tensor_tensor(
                vci[:, b_ * NG:(b_ + 1) * NG], sC, sD, OP.add), waits=[e_c, e_d], inc="v")
            ev_vinit.append((evr, evi))
        ev_vinit_done = [ev_vinit[-1][0], ev_vinit[-1][1]]

        ev_base_last = None
        for g in range(NG):
            phig = phi[:, g:g + 1]
            dblc = s2[:, 0:TSEG]
            dbls = s3[:, 0:TSEG]
            w0 = [e_phi, ev_small] + ev_rot_done
            if ev_base_last is not None:
                w0.append(ev_base_last)
            e_m = pl.add("sc", lambda e, phig=phig: e.activation(s4[:, 0:8], io8[:], AF.Identity,
                                                                 bias=0.0, scale=phig),
                         waits=w0, inc="a")
            e_ss = pl.add("sc", lambda e: e.activation(dbls[:, 0:8], s4[:, 0:8], AF.Sin, bias=0.0, scale=1.0),
                          waits=[e_m], inc="a")
            e_cc = pl.add("sc", lambda e: e.activation(dblc[:, 0:8], s4[:, 0:8], AF.Sin,
                                                       bias=cst[:, C_PIH:C_PIH + 1], scale=1.0),
                          waits=[e_m], inc="a")
            last = [e_ss, e_cc]
            n = 8
            while n < TSEG:
                rcn = rc[n][:, g:g + 1]
                rsn = rs[n][:, g:g + 1]
                e_t = pl.add("ve", lambda e, n=n, rsn=rsn, dbls=dbls: e.tensor_scalar(
                    s4[:, 0:n], dbls[:, 0:n], rsn, None, OP.mult), waits=last, inc="v")
                e_nc = pl.add("ve", lambda e, n=n, rcn=rcn, dblc=dblc: e.scalar_tensor_tensor(
                    dblc[:, n:2 * n], dblc[:, 0:n], rcn, s4[:, 0:n], OP.mult, OP.subtract),
                    waits=[e_t], inc="v")
                e_t2 = pl.add("ve", lambda e, n=n, rcn=rcn, dbls=dbls: e.tensor_scalar(
                    s4[:, 0:n], dbls[:, 0:n], rcn, None, OP.mult), waits=[e_nc], inc="v")
                e_ns = pl.add("ve", lambda e, n=n, rsn=rsn, dblc=dblc, dbls=dbls: e.scalar_tensor_tensor(
                    dbls[:, n:2 * n], dblc[:, 0:n], rsn, s4[:, 0:n], OP.mult, OP.add),
                    waits=[e_t2], inc="v")
                last = [e_nc, e_ns]
                n *= 2
            e_bcg = pl.add("ve", lambda e, g=g, dblc=dblc: e.tensor_copy(bc[g][:], dblc), waits=last, inc="v")
            e_bsg = pl.add("ve", lambda e, g=g, dbls=dbls: e.tensor_copy(bs[g][:], dbls), waits=[e_bcg], inc="v")
            ev_base_last = e_bsg
        ev_base_done = ev_base_last

        if upto == "tbl":
            for g in range(NG):
                pl.dma("gp", out_d[g * 128:(g + 1) * 128, 0, 0:TSEG], bc[g][:], waits=[ev_base_done], sem="st")
                pl.dma("gp", out_d[g * 128:(g + 1) * 128, 0, TSEG:2 * TSEG], bs[g][:], waits=[ev_base_done], sem="st")
            total_store = pl.cnt["st"]
            pl.add("sy", None, waits=[("st", total_store)])
            def run_queue(eng_obj, name):
                hw = {s: 0 for s in SEMS}
                for (fn, waits, inc, n) in pl.q[name]:
                    for (s, val) in waits:
                        if val > hw[s]:
                            eng_obj.wait_ge(sem[s], val)
                            hw[s] = val
                    if fn is None:
                        continue
                    r = fn(eng_obj)
                    if inc is not None:
                        r.then_inc(sem[inc], n)
            @block.sync
            def _(sync):
                run_queue(sync, "sy")
            @block.vector
            def _(vector):
                run_queue(vector, "ve")
            @block.scalar
            def _(scalar):
                run_queue(scalar, "sc")
            @block.gpsimd
            def _(gpsimd):
                run_queue(gpsimd, "gp")
            @block.tensor
            def _(tensor):
                run_queue(tensor, "te")
            return nc

        pl.region = "su2"
        ev_w1 = ev_base_done
        for g in range(NG):
            for qtr in range(4):
                ev_l = pl.dma("sy", s2[:, 0:1024],
                              w1t_d[g * 128:(g + 1) * 128, qtr * 1024:(qtr + 1) * 1024],
                              waits=[ev_w1])
                ev_w1 = pl.add("ve", lambda e, g=g, qtr=qtr: e.tensor_copy(
                    w1b[g][:, qtr * 1024:(qtr + 1) * 1024], s2[:, 0:1024]), waits=[ev_l], inc="v")

        ev_w2 = None
        ev_src_free = pl.nop("ve", [ev_w1, ev_kt, ev_vinit_done[0], ev_vinit_done[1]])
        for c in range(NHC):
            ev_l = pl.dma("sy", s1[:, 0:1024], w2t_d[c * 128:(c + 1) * 128, :], waits=[ev_src_free])
            ready = pl.nop("ve", [ev_l])
            ev_w2 = emit_quant(pl, s1[:, 0:1024], w2q[c][:], 1024, True, True,
                               cst, kt, s2, s3, s4, ready)
            ev_src_free = ev_w2
        ready = pl.nop("ve", [ev_w2, ev_small])
        ev_bq = emit_quant(pl, b2c[:], bqt[:], NG, True, False, cst, kt, s2, s3, s4, ready)

        pl.region = "main"
        ev_xfree = ev_bq
        ev_xb_free = ev_bq
        ev_psA_free = None
        ev_psB_free = None
        ev_psBC_free = None
        ev_hq_free = None
        ev_oq_free = None
        ev_psX_free = None
        ev_psQ_free = None
        ev_h4_free = None
        ev_sr_prep = None
        ev_rotors = []
        ev_carry = {}
        for g in range(NG):
            for b_ in range(BPC):
                ev_carry[("r", g, b_)] = ev_vinit[b_][0]
                ev_carry[("i", g, b_)] = ev_vinit[b_][1]
        ev_rot_free = [None] * NG
        ev_scratch = ev_bq

        segs = [(si, b_) for si in range(NSEG_PER_B) for b_ in range(BPC)]
        for (si, b_) in segs:
            t0 = si * TSEG
            if b_ == 0:
                w8 = [ev_kt, e_gte] + ev_sr
                if ev_sr_prep is not None:
                    w8.append(ev_sr_prep)
                e_ = pl.add("ve", lambda e: e.tensor_tensor(tmp8a[:], ire[:], src_t[:], OP.mult),
                            waits=w8 + ev_rot_done, inc="v")
                e_ = pl.add("ve", lambda e: e.tensor_tensor(tmp8b[:], iim[:], srs_t[:], OP.mult),
                            waits=[e_], inc="v")
                e_pa = pl.add("ve", lambda e: e.tensor_tensor(pa_t[:], tmp8a[:], tmp8b[:], OP.add),
                              waits=[e_], inc="v")
                e_ = pl.add("ve", lambda e: e.tensor_tensor(tmp8a[:], iim[:], src_t[:], OP.mult),
                            waits=[e_pa], inc="v")
                e_ = pl.add("ve", lambda e: e.tensor_tensor(tmp8b[:], ire[:], srs_t[:], OP.mult),
                            waits=[e_], inc="v")
                e_pb = pl.add("ve", lambda e: e.tensor_tensor(pb_t[:], tmp8a[:], tmp8b[:], OP.subtract),
                              waits=[e_], inc="v")
                e_qa = pl.add("ve", lambda e: e.tensor_tensor(qa_t[:], gte[:], src_t[:], OP.mult),
                              waits=[e_pb], inc="v")
                e_qb = pl.add("ve", lambda e: e.tensor_tensor(qb_t[:], gte[:], srs_t[:], OP.mult),
                              waits=[e_pb], inc="v")
                ev_rotors = [e_pa, e_pb, e_qa, e_qb]
                if upto == "rotors" and si == tap_si:
                    pl.dma("gp", out_d[0:128, 0, 0:NG], pa_t[:], waits=[e_pa], sem="st")
                    pl.dma("gp", out_d[0:128, 0, NG:2 * NG], pb_t[:], waits=[e_pb], sem="st")
                    pl.dma("gp", out_d[0:128, 0, 2 * NG:3 * NG], src_t[:], waits=[e_qb], sem="st")
                    pl.dma("gp", out_d[0:128, 0, 3 * NG:4 * NG], srs_t[:], waits=[e_qb], sem="st")
            ev_xl = []
            for g in range(NG):
                w = [ev_xfree, ev_rot_free[g]]
                ev_xl.append(pl.dma("sy", xseg[g][:],
                                    x_d[g * 128:(g + 1) * 128, b_, t0:t0 + TSEG], waits=w))
            ev_post = []
            for g in range(NG):
                pa = pa_t[:, g:g + 1]
                pb = pb_t[:, g:g + 1]
                qa = qa_t[:, g:g + 1]
                qb = qb_t[:, g:g + 1]
                w0 = [ev_xl[-1], ev_scratch] + ev_rotors
                e_xc = pl.add("ve", lambda e, g=g: e.tensor_tensor(s2[:, 0:TSEG], xseg[g][:], bc[g][:], OP.mult),
                              waits=w0, inc="v")
                e_xs = pl.add("ve", lambda e, g=g: e.tensor_tensor(s3[:, 0:TSEG], xseg[g][:], bs[g][:], OP.mult),
                              waits=w0, inc="v")
                e_t = pl.add("ve", lambda e, pb=pb: e.tensor_scalar(s4[:, 0:TSEG], s3[:, 0:TSEG], pb, None, OP.mult),
                             waits=[e_xs], inc="v")
                wwr = [e_t, ev_rot_free[g]]
                e_wr = pl.add("ve", lambda e, g=g, pa=pa: e.scalar_tensor_tensor(
                    wr[g][:], s2[:, 0:TSEG], pa, s4[:, 0:TSEG], OP.mult, OP.add), waits=wwr, inc="v")
                e_t2 = pl.add("ve", lambda e, pa=pa: e.tensor_scalar(s4[:, 0:TSEG], s3[:, 0:TSEG], pa, None, OP.mult),
                              waits=[e_wr], inc="v")
                e_wi = pl.add("ve", lambda e, g=g, pb=pb: e.scalar_tensor_tensor(
                    wi[g][:], s2[:, 0:TSEG], pb, s4[:, 0:TSEG], OP.mult, OP.subtract), waits=[e_t2], inc="v")
                if upto == "rot0" and si == tap_si and b_ == tap_b:
                    pl.dma("gp", out_d[g * 128:(g + 1) * 128, 0, 0:TSEG], wr[g][:], waits=[e_wr], sem="st")
                    pl.dma("gp", out_d[g * 128:(g + 1) * 128, 0, TSEG:2 * TSEG], wi[g][:], waits=[e_wi], sem="st")
                rho_b = rho[:, g:g + 1].broadcast_to([128, TSEG])
                e_scr = pl.add("ve", lambda e, g=g, rho_b=rho_b, b_=b_: e.tensor_tensor_scan(
                    wr[g][:], rho_b, wr[g][:], vcr[:, b_ * NG + g:b_ * NG + g + 1], OP.mult, OP.add),
                    waits=[e_wr, e_rho, ev_carry[("r", g, b_)]], inc="v")
                e_sci = pl.add("ve", lambda e, g=g, rho_b=rho_b, b_=b_: e.tensor_tensor_scan(
                    wi[g][:], rho_b, wi[g][:], vci[:, b_ * NG + g:b_ * NG + g + 1], OP.mult, OP.add),
                    waits=[e_wi, ev_carry[("i", g, b_)]], inc="v")
                e_cr = pl.add("ve", lambda e, g=g, b_=b_: e.tensor_copy(
                    vcr[:, b_ * NG + g:b_ * NG + g + 1], wr[g][:, TSEG - 1:TSEG]), waits=[e_scr], inc="v")
                e_ci = pl.add("ve", lambda e, g=g, b_=b_: e.tensor_copy(
                    vci[:, b_ * NG + g:b_ * NG + g + 1], wi[g][:, TSEG - 1:TSEG]), waits=[e_sci], inc="v")
                ev_carry[("r", g, b_)] = e_cr
                ev_carry[("i", g, b_)] = e_ci
                if upto == "scan0" and si == tap_si and b_ == tap_b:
                    pl.dma("gp", out_d[g * 128:(g + 1) * 128, 0, 0:TSEG], wr[g][:], waits=[e_scr], sem="st")
                    pl.dma("gp", out_d[g * 128:(g + 1) * 128, 0, TSEG:2 * TSEG], wi[g][:], waits=[e_sci], sem="st")
                e_u1 = pl.add("ve", lambda e, g=g, qa=qa: e.tensor_scalar(
                    s2[:, 0:TSEG], wr[g][:], qa, None, OP.mult), waits=[e_scr], inc="v")
                e_f1 = pl.add("ve", lambda e, g=g, qb=qb: e.scalar_tensor_tensor(
                    s2[:, 0:TSEG], wi[g][:], qb, s2[:, 0:TSEG], OP.mult, OP.subtract),
                    waits=[e_u1, e_sci], inc="v")
                e_u3 = pl.add("ve", lambda e, g=g, qb=qb: e.tensor_scalar(
                    s3[:, 0:TSEG], wr[g][:], qb, None, OP.mult), waits=[e_f1], inc="v")
                e_f2 = pl.add("ve", lambda e, g=g, qa=qa: e.scalar_tensor_tensor(
                    s3[:, 0:TSEG], wi[g][:], qa, s3[:, 0:TSEG], OP.mult, OP.add), waits=[e_u3], inc="v")
                ev_rot_free[g] = e_f2
                e_n1 = pl.add("ve", lambda e, g=g: e.tensor_tensor(
                    s2[:, 0:TSEG], s2[:, 0:TSEG], bc[g][:], OP.mult), waits=[e_f2], inc="v")
                e_n2 = pl.add("ve", lambda e, g=g: e.tensor_tensor(
                    s3[:, 0:TSEG], s3[:, 0:TSEG], bs[g][:], OP.mult), waits=[e_n1], inc="v")
                e_md = pl.add("ve", lambda e: e.tensor_tensor(
                    s2[:, 0:TSEG], s2[:, 0:TSEG], s3[:, 0:TSEG], OP.add), waits=[e_n2], inc="v")
                e_res = pl.add("ve", lambda e, g=g: e.tensor_tensor(
                    xseg[g][:], xseg[g][:], s2[:, 0:TSEG], OP.subtract), waits=[e_md], inc="v")
                if upto == "res0" and si == tap_si and b_ == tap_b:
                    pl.dma("gp", out_d[g * 128:(g + 1) * 128, 0, 0:TSEG], xseg[g][:], waits=[e_res], sem="st")
                ev_post.append(e_res)
            ev_scratch = ev_post[-1]
            if upto in ("rot0", "scan0", "rotors", "res0") and si == tap_si and b_ == tap_b:
                break

            ev_sq = []
            for g in range(NG):
                ev_sq.append(pl.add("sc", lambda e, g=g: e.activation(
                    xb[g][:], xseg[g][:], AF.Square, bias=0.0, scale=1.0),
                    waits=[ev_post[g], ev_xb_free], inc="a"))
            e_mmq = None
            for g in range(NG):
                w = [ev_sq[g], ev_onebc]
                if g == 0 and ev_psQ_free is not None:
                    w.append(ev_psQ_free)
                e_mmq = pl.add("te", lambda e, g=g: e.matmul(
                    psQ[:], onebc[:], xb[g][:], start=(g == 0), stop=(g == NG - 1)),
                    waits=w, inc=("t" if g == NG - 1 else None))
            ev_cpx = []
            for g in range(NG):
                ev_cpx.append(pl.add("ve", lambda e, g=g: e.tensor_copy(xb[g][:], xseg[g][:]),
                                     waits=[e_mmq, ev_post[g]], inc="v"))
            e_mmx = None
            for g in range(NG):
                w = [ev_cpx[g]]
                if g == 0 and ev_psX_free is not None:
                    w.append(ev_psX_free)
                e_mmx = pl.add("te", lambda e, g=g: e.matmul(
                    psX[:], onebc[:], xb[g][:], start=(g == 0), stop=(g == NG - 1)),
                    waits=w, inc=("t" if g == NG - 1 else None))
            e_ = pl.add("ve", lambda e: e.tensor_scalar(stm[:], psX[:], 1.0 / D, None, OP.mult),
                        waits=[e_mmx], inc="v")
            ev_psX_free = e_
            e_ = pl.add("ve", lambda e: e.tensor_scalar(stq[:], psQ[:], 1.0 / D, None, OP.mult),
                        waits=[e_, e_mmq], inc="v")
            ev_psQ_free = e_
            e_ = pl.add("ve", lambda e: e.tensor_tensor(stv[:], stm[:], stm[:], OP.mult),
                        waits=[e_], inc="v")
            e_var = pl.add("ve", lambda e: e.tensor_tensor(stv[:], stq[:], stv[:], OP.subtract),
                           waits=[e_], inc="v")
            e_srt = pl.add("sc", lambda e: e.activation(stv[:], stv[:], AF.Sqrt,
                                                        bias=cst[0:1, C_LNEPS:C_LNEPS + 1], scale=1.0),
                           waits=[e_var], inc="a")
            e_rstd = pl.add("ve", lambda e: e.reciprocal(str_[:], stv[:]), waits=[e_srt], inc="v")
            e_cb = pl.add("ve", lambda e: e.tensor_tensor(stc[:], stm[:], str_[:], OP.mult),
                          waits=[e_rstd], inc="v")
            e_cb = pl.add("ve", lambda e: e.tensor_scalar(stc[:], stc[:], -1.0, None, OP.mult),
                          waits=[e_cb], inc="v")
            wbc = [e_rstd, e_cb, ev_small]
            if ev_psBC_free is not None:
                wbc.append(ev_psBC_free)
            e_bca = pl.add("te", lambda e: e.matmul(psBC[:, 0:512], oner[0:1, :], str_[:],
                                                    start=True, stop=True), waits=wbc, inc="t")
            e_bcb = pl.add("te", lambda e: e.matmul(psBC[:, 512:1024], oner[0:1, :], stc[:],
                                                    start=True, stop=True), waits=[e_bca], inc="t")
            ev_ln = []
            for g in range(NG):
                e_ = pl.add("ve", lambda e, g=g: e.tensor_tensor(
                    s2[:, 0:TSEG], xseg[g][:], psBC[:, 0:512], OP.mult),
                    waits=[e_bcb, ev_cpx[g]], inc="v")
                e_ = pl.add("ve", lambda e, g=g: e.tensor_tensor(
                    xseg[g][:], s2[:, 0:TSEG], psBC[:, 512:1024], OP.add), waits=[e_], inc="v")
                e_ = pl.add("ve", lambda e, g=g: e.tensor_copy(xb[g][:], xseg[g][:]), waits=[e_], inc="v")
                ev_ln.append(e_)
            ev_psBC_free = ev_ln[-1]

            if upto == "ln1":
                last_store = None
                for g in range(NG):
                    last_store = pl.dma("sy", out_d[g * 128:(g + 1) * 128, b_, t0:t0 + TSEG],
                                        xseg[g][:], waits=[ev_ln[g]], sem="st")
                ev_xfree = last_store
                ev_xb_free = ev_ln[-1]
                if b_ == BPC - 1 and si < NSEG_PER_B - 1:
                    wsr = ev_rot_done + ev_sr + ev_rotors
                    if ev_sr_prep is not None:
                        wsr.append(ev_sr_prep)
                    e_ = pl.add("ve", lambda e: e.tensor_tensor(tmp8a[:], src_t[:], rc[512][:], OP.mult),
                                waits=wsr, inc="v")
                    e_ = pl.add("ve", lambda e: e.tensor_tensor(tmp8b[:], srs_t[:], rs[512][:], OP.mult),
                                waits=[e_], inc="v")
                    e_nc2 = pl.add("ve", lambda e: e.tensor_tensor(tmp8a[:], tmp8a[:], tmp8b[:], OP.subtract),
                                   waits=[e_], inc="v")
                    e_ = pl.add("ve", lambda e: e.tensor_tensor(tmp8b[:], src_t[:], rs[512][:], OP.mult),
                                waits=[e_nc2], inc="v")
                    e_ns2 = pl.add("ve", lambda e: e.tensor_tensor(s4[:, 0:NG], srs_t[:], rc[512][:], OP.mult),
                                   waits=[e_], inc="v")
                    e_s2u = pl.add("ve", lambda e: e.tensor_tensor(srs_t[:], s4[:, 0:NG], tmp8b[:], OP.add),
                                   waits=[e_ns2], inc="v")
                    e_c2u = pl.add("ve", lambda e: e.tensor_copy(src_t[:], tmp8a[:]), waits=[e_s2u], inc="v")
                    ev_sr_prep = e_c2u
                continue

            last_fin = None
            last_store = None
            for f in range(NSUB):
                c0, c1c = f * FS, (f + 1) * FS
                ev_q = None
                for cg in range(NCG):
                    e_mm = None
                    for j in range(CGN):
                        hc = cg * CGN + j
                        for kg in range(NG):
                            w = [ev_ln[kg], ev_w1]
                            if ev_psA_free is not None and j == 0 and kg == 0:
                                w.append(ev_psA_free)
                            e_mm = pl.add("te", lambda e, j=j, hc=hc, kg=kg, c0=c0, c1c=c1c: e.matmul(
                                psA[:, j * FS:(j + 1) * FS], w1b[kg][:, hc * 128:(hc + 1) * 128],
                                xb[kg][:, c0:c1c], start=(kg == 0), stop=(kg == NG - 1)),
                                waits=w, inc=("t" if (kg == NG - 1 and j == CGN - 1) else None))
                    ev_ev = []
                    for j in range(CGN):
                        hc = cg * CGN + j
                        w = [e_mm]
                        if ev_q is not None:
                            w.append(ev_q)
                        elif ev_h4_free is not None:
                            w.append(ev_h4_free)
                        ev_ev.append(pl.add("sc", lambda e, j=j, hc=hc: e.activation(
                            h4[:, j * FS:(j + 1) * FS], psA[:, j * FS:(j + 1) * FS], AF.Relu,
                            bias=b1c[:, hc:hc + 1], scale=1.0), waits=w, inc="a"))
                    ev_psA_free = ev_ev[-1]
                    w_hq = [ev_hq_free] if (ev_hq_free is not None and cg == 0) else []
                    w_h4 = [ev_h4_free] if (ev_h4_free is not None and ev_q is None) else []
                    ready = pl.nop("ve", [ev_ev[-1]] + w_hq + w_h4)
                    ev_q = emit_quant(pl, h4[:, 0:1024], hq[:, cg * 1024:(cg + 1) * 1024], 1024,
                                      False, True, cst, kt, s2, s3, s4, ready)
                ev_h4_free = ev_q
                ev_ev2 = []
                for wv in range(2):
                    e_mm2 = None
                    for dj in range(4):
                        dg = wv * 4 + dj
                        for hc in range(NHC):
                            w = [ev_q, ev_w2]
                            if ev_psB_free is not None and dj == 0 and hc == 0:
                                w.append(ev_psB_free)
                            e_mm2 = pl.add("te", lambda e, dj=dj, dg=dg, hc=hc: e.matmul(
                                psB[:, dj * FS:(dj + 1) * FS], w2q[hc][:, dg * 128:(dg + 1) * 128],
                                hq[:, hc * FS:(hc + 1) * FS], start=(hc == 0), stop=(hc == NHC - 1)),
                                waits=w, inc=("t" if (hc == NHC - 1 and dj == 3) else None))
                    for dj in range(4):
                        dg = wv * 4 + dj
                        w = [e_mm2, ev_bq]
                        if ev_oq_free is not None and wv == 0 and dj == 0:
                            w.append(ev_oq_free)
                        ev_ev2.append(pl.add("sc", lambda e, dj=dj, dg=dg: e.activation(
                            oq[:, dg * FS:(dg + 1) * FS], psB[:, dj * FS:(dj + 1) * FS], AF.Identity,
                            bias=bqt[:, dg:dg + 1], scale=1.0), waits=w, inc="a"))
                    ev_psB_free = ev_ev2[-1]
                ev_hq_free = ev_ev2[-1]
                ev_oqq = []
                for sl_i in range(2):
                    ready = pl.nop("ve", ev_ev2[4 * sl_i:4 * sl_i + 4])
                    ev_oqq.append(emit_quant(pl, oq[:, sl_i * 1024:(sl_i + 1) * 1024],
                                             oq[:, sl_i * 1024:(sl_i + 1) * 1024], 1024,
                                             True, False, cst, kt, s2, s3, s4, ready))
                ev_r2 = []
                for g in range(NG):
                    ev_r2.append(pl.add("ve", lambda e, g=g, c0=c0, c1c=c1c: e.tensor_tensor(
                        oq[:, g * FS:(g + 1) * FS], xseg[g][:, c0:c1c], oq[:, g * FS:(g + 1) * FS], OP.add),
                        waits=[ev_oqq[g // 4], ev_ln[g]], inc="v"))
                e_mmq2 = None
                e_mmx2 = None
                for g in range(NG):
                    half = s2 if (g % 2 == 0) else s3
                    e_sq2 = pl.add("sc", lambda e, g=g, half=half: e.activation(
                        half[:, 0:FS], oq[:, g * FS:(g + 1) * FS], AF.Square, bias=0.0, scale=1.0),
                        waits=[ev_r2[g]] + ([e_mmq2] if e_mmq2 else [ev_r2[-1]]), inc="a")
                    wq2 = [e_sq2]
                    wx2 = [ev_r2[g]]
                    if g == 0:
                        if ev_psQ_free is not None:
                            wq2.append(ev_psQ_free)
                        if ev_psX_free is not None:
                            wx2.append(ev_psX_free)
                    e_mmq2 = pl.add("te", lambda e, g=g, half=half: e.matmul(
                        psQ[:, 0:FS], onec[:], half[:, 0:FS], start=(g == 0), stop=(g == NG - 1)),
                        waits=wq2, inc="t")
                    e_mmx2 = pl.add("te", lambda e, g=g: e.matmul(
                        psX[:, 0:FS], onec[:], oq[:, g * FS:(g + 1) * FS], start=(g == 0), stop=(g == NG - 1)),
                        waits=wx2, inc=("t" if g == NG - 1 else None))
                e_ = pl.add("ve", lambda e: e.tensor_scalar(stm[:, 0:FS], psX[:, 0:FS], 1.0 / D, None, OP.mult),
                            waits=[e_mmx2], inc="v")
                ev_psX_free = e_
                e_ = pl.add("ve", lambda e: e.tensor_scalar(stq[:, 0:FS], psQ[:, 0:FS], 1.0 / D, None, OP.mult),
                            waits=[e_, e_mmq2], inc="v")
                ev_psQ_free = e_
                e_ = pl.add("ve", lambda e: e.tensor_tensor(stv[:, 0:FS], stm[:, 0:FS], stm[:, 0:FS], OP.mult),
                            waits=[e_], inc="v")
                e_var2 = pl.add("ve", lambda e: e.tensor_tensor(stv[:, 0:FS], stq[:, 0:FS], stv[:, 0:FS], OP.subtract),
                                waits=[e_], inc="v")
                e_srt2 = pl.add("sc", lambda e: e.activation(stv[:, 0:FS], stv[:, 0:FS], AF.Sqrt,
                                                             bias=cst[0:1, C_LNEPS:C_LNEPS + 1], scale=1.0),
                                waits=[e_var2], inc="a")
                e_rstd2 = pl.add("ve", lambda e: e.reciprocal(str_[:, 0:FS], stv[:, 0:FS]),
                                 waits=[e_srt2], inc="v")
                e_cb2 = pl.add("ve", lambda e: e.tensor_tensor(stc[:, 0:FS], stm[:, 0:FS], str_[:, 0:FS], OP.mult),
                               waits=[e_rstd2], inc="v")
                e_cb2 = pl.add("ve", lambda e: e.tensor_scalar(stc[:, 0:FS], stc[:, 0:FS], -1.0, None, OP.mult),
                               waits=[e_cb2], inc="v")
                e_bca2 = pl.add("te", lambda e: e.matmul(psBC[:, 0:FS], oner[0:1, :], str_[:, 0:FS],
                                                         start=True, stop=True),
                                waits=[e_rstd2, e_cb2, ev_psBC_free], inc="t")
                e_bcb2 = pl.add("te", lambda e: e.matmul(psBC[:, 512:512 + FS], oner[0:1, :], stc[:, 0:FS],
                                                         start=True, stop=True), waits=[e_bca2], inc="t")
                ev_fin = []
                for g in range(NG):
                    e_ = pl.add("ve", lambda e, g=g: e.tensor_tensor(
                        s4[:, 0:FS], oq[:, g * FS:(g + 1) * FS], psBC[:, 0:FS], OP.mult),
                        waits=[e_bcb2], inc="v")
                    e_ = pl.add("ve", lambda e, g=g: e.tensor_tensor(
                        oq[:, g * FS:(g + 1) * FS], s4[:, 0:FS], psBC[:, 512:512 + FS], OP.add),
                        waits=[e_], inc="v")
                    ev_fin.append(e_)
                ev_psBC_free = ev_fin[-1]
                for g in range(NG):
                    last_store = pl.dma("sy", out_d[g * 128:(g + 1) * 128, b_, t0 + c0:t0 + c1c],
                                        oq[:, g * FS:(g + 1) * FS], waits=[ev_fin[g]], sem="st")
                ev_oq_free = last_store
                last_fin = ev_fin[-1]
            ev_xfree = last_fin
            ev_xb_free = last_fin

            if b_ == BPC - 1 and si < NSEG_PER_B - 1:
                wsr = ev_rot_done + ev_sr + ev_rotors
                if ev_sr_prep is not None:
                    wsr.append(ev_sr_prep)
                e_ = pl.add("ve", lambda e: e.tensor_tensor(tmp8a[:], src_t[:], rc[512][:], OP.mult),
                            waits=wsr, inc="v")
                e_ = pl.add("ve", lambda e: e.tensor_tensor(tmp8b[:], srs_t[:], rs[512][:], OP.mult),
                            waits=[e_], inc="v")
                e_nc2 = pl.add("ve", lambda e: e.tensor_tensor(tmp8a[:], tmp8a[:], tmp8b[:], OP.subtract),
                               waits=[e_], inc="v")
                e_ = pl.add("ve", lambda e: e.tensor_tensor(tmp8b[:], src_t[:], rs[512][:], OP.mult),
                            waits=[e_nc2], inc="v")
                e_ns2 = pl.add("ve", lambda e: e.tensor_tensor(s4[:, 0:NG], srs_t[:], rc[512][:], OP.mult),
                               waits=[e_], inc="v")
                e_s2u = pl.add("ve", lambda e: e.tensor_tensor(srs_t[:], s4[:, 0:NG], tmp8b[:], OP.add),
                               waits=[e_ns2], inc="v")
                e_c2u = pl.add("ve", lambda e: e.tensor_copy(src_t[:], tmp8a[:]), waits=[e_s2u], inc="v")
                ev_sr_prep = e_c2u

        total_store = pl.cnt["st"]
        pl.add("sy", None, waits=[("st", total_store)])

        def run_queue(eng_obj, name):
            hw = {s: 0 for s in SEMS}
            for (fn, waits, inc, n) in pl.q[name]:
                for (s, val) in waits:
                    if val > hw[s]:
                        eng_obj.wait_ge(sem[s], val)
                        hw[s] = val
                if fn is None:
                    continue
                r = fn(eng_obj)
                if inc is not None:
                    r.then_inc(sem[inc], n)

        @block.sync
        def _(sync):
            run_queue(sync, "sy")

        @block.vector
        def _(vector):
            run_queue(vector, "ve")

        @block.scalar
        def _(scalar):
            run_queue(scalar, "sc")

        @block.gpsimd
        def _(gpsimd):
            run_queue(gpsimd, "gp")

        @block.tensor
        def _(tensor):
            run_queue(tensor, "te")

    return nc


_NC_CACHE = None


def kernel(L0_attn_x, L0_attn_phazor_init_re, L0_attn_phazor_init_im,
           L0_attn_phazor_angle, L0_attn_ln_minus_ln_phazor_abs,
           L0_attn_last_conv_re, L0_attn_last_conv_im, L0_attn_ln_phazor_scale,
           L0_ffn_w1, L0_ffn_b1, L0_ffn_w2, L0_ffn_b2):
    global _NC_CACHE
    x = np.ascontiguousarray(np.asarray(L0_attn_x, dtype=np.float32))
    w1 = np.ascontiguousarray(np.asarray(L0_ffn_w1, dtype=np.float32))
    b1 = np.ascontiguousarray(np.asarray(L0_ffn_b1, dtype=np.float32))
    w2 = np.ascontiguousarray(np.asarray(L0_ffn_w2, dtype=np.float32))
    b2 = np.ascontiguousarray(np.asarray(L0_ffn_b2, dtype=np.float32))

    def chan(v):
        return np.ascontiguousarray(np.asarray(v, dtype=np.float32).reshape(NG, 128).T)

    w1t = np.ascontiguousarray(w1.T)
    w2t = np.ascontiguousarray(w2.T)
    b1c = np.ascontiguousarray(b1.reshape(NHC, 128).T)
    b2c = chan(b2)
    cstv = np.ascontiguousarray(np.broadcast_to(CONST_VALUES, (128, NCONST)))
    io8 = np.ascontiguousarray(np.broadcast_to(np.arange(8, dtype=np.float32), (128, 8)))
    onec = np.ones((128, 1), dtype=np.float32)
    oner = np.ones((1, 128), dtype=np.float32)

    shared = dict(w1t=w1t, w2t=w2t, b1c=b1c, b2c=b2c,
                  angle=chan(L0_attn_phazor_angle), lnml=chan(L0_attn_ln_minus_ln_phazor_abs),
                  scl=chan(L0_attn_ln_phazor_scale), ire=chan(L0_attn_phazor_init_re),
                  iim=chan(L0_attn_phazor_init_im), lre=chan(L0_attn_last_conv_re),
                  lim=chan(L0_attn_last_conv_im), cst=cstv, iota8=io8, onec=onec, oner=oner)

    in_maps = []
    for c in range(NCORES):
        xc = np.ascontiguousarray(x[c * BPC:(c + 1) * BPC].transpose(2, 0, 1))
        m = dict(shared)
        m["x"] = xc
        in_maps.append(m)

    if _NC_CACHE is None:
        _NC_CACHE = build_nc()
    res = run_bass_kernel_spmd(_NC_CACHE, in_maps, core_ids=list(range(NCORES)))
    out = np.empty((B, L, D), dtype=np.float32)
    for c in range(NCORES):
        oc = np.asarray(res.results[c]["out"])
        out[c * BPC:(c + 1) * BPC] = oc.transpose(1, 2, 0)
    return out
